# version 25
# baseline (speedup 1.0000x reference)
import numpy as np
import ml_dtypes
import concourse.bass as bass
import concourse.mybir as mybir
from concourse.bass_utils import run_bass_kernel_spmd
from contextlib import ExitStack

F32 = mybir.dt.float32
BF16 = mybir.dt.bfloat16
AF = mybir.ActivationFunctionType
ALU = mybir.AluOpType
AX = mybir.AxisListType.X

P = 128
D = 2048
NCH = 16
TL = 1024
NT = 8
NS = 4
NE = 8
EPS_LN = 1e-5
EPS_RMS = 1e-6
EPS_CLIP = 1e-6
BIG = 1e30


class Sem:
    def __init__(self, h):
        self.h = h
        self.n = 0

    def bump(self, k=1):
        self.n += k
        return self.n


def build_nc():
    nc = bass.Bass("TRN2", target_bir_lowering=False, debug=False)

    xa = nc.dram_tensor("xa", [TL, D], F32, kind="ExternalInput")
    xb = nc.dram_tensor("xb", [TL, D], F32, kind="ExternalInput")
    WqR = nc.dram_tensor("WqR", [NCH, P, D], F32, kind="ExternalInput")
    WkR = nc.dram_tensor("WkR", [NS, P, NCH * 512], F32, kind="ExternalInput")
    WvR = nc.dram_tensor("WvR", [NS, P, NCH * 512], F32, kind="ExternalInput")
    WoR = nc.dram_tensor("WoR", [NS, P, NCH * 512], F32, kind="ExternalInput")
    PjR = nc.dram_tensor("PjR", [NS, P, NCH * 512], F32, kind="ExternalInput")
    WeR = nc.dram_tensor("WeR", [NE, NS, P, NCH * 512], F32, kind="ExternalInput")
    rWr = nc.dram_tensor("rWr", [P, NCH * 8], F32, kind="ExternalInput")
    rbias = nc.dram_tensor("rbias", [1, 8], F32, kind="ExternalInput")
    ebias = nc.dram_tensor("ebias", [NE, D], F32, kind="ExternalInput")
    pjbias = nc.dram_tensor("pjbias", [1, D], F32, kind="ExternalInput")
    obias = nc.dram_tensor("obias", [1, D], F32, kind="ExternalInput")
    nwg = nc.dram_tensor("nwg", [1, D], F32, kind="ExternalInput")
    ident = nc.dram_tensor("ident", [P, P], BF16, kind="ExternalInput")
    out = nc.dram_tensor("out", [TL, D], F32, kind="ExternalOutput")

    xn_dram = nc.dram_tensor("xn_dram", [2 * TL, D], BF16)

    es = ExitStack()
    sbt = lambda name, shape, dt: es.enter_context(nc.sbuf_tensor(name, shape, dt))
    pst = lambda name, shape, dt: es.enter_context(nc.psum_tensor(name, shape, dt))

    xnT = sbt("xnT", [P, NCH * TL], BF16)
    qT = sbt("qT", [P, NCH * TL], BF16)
    comb = sbt("comb", [P, NT * D], BF16)
    wb = [sbt("wb%d" % i, [P, NCH * 512], BF16) for i in range(2)]
    x_t = [sbt("x_t%d" % i, [P, D], F32) for i in range(2)]
    fo_t = [sbt("fo_t%d" % i, [P, D], F32) for i in range(2)]
    xn_t = [sbt("xn_t%d" % i, [P, D], BF16) for i in range(2)]
    kvks = sbt("kvks", [P, NCH * 129], BF16)
    kvq = sbt("kvq", [P, NCH * 130], BF16)
    nwb = sbt("nwb", [P, D], BF16)
    rW_sb = sbt("rW_sb", [P, NCH * 8], BF16)
    row_a = sbt("row_a", [1, D], BF16)
    row_b = sbt("row_b", [1, D], BF16)
    rb_sb = sbt("rb_sb", [1, 8], BF16)
    silu_t = [sbt("silu%d" % i, [P, 512], BF16) for i in range(2)]
    xn_sl = [sbt("xnsl%d" % i, [P, 512], BF16) for i in range(2)]
    att_sl = [sbt("attsl%d" % i, [P, 512], F32) for i in range(2)]
    id_sb = sbt("id_sb", [P, P], BF16)
    ones_r = sbt("ones_r", [1, P], BF16)
    ones_c = sbt("ones_c", [P, 1], BF16)
    gates = sbt("gates", [P, NT * 8], F32)
    bst = [sbt("bst%d" % i, [P, 24], F32) for i in range(2)]
    mv = [sbt("mv%d" % i, [P, 2], F32) for i in range(2)]
    rstd1 = [sbt("rstd1_%d" % i, [P, 1], F32) for i in range(2)]
    rstd2 = [sbt("rstd2_%d" % i, [P, 1], F32) for i in range(2)]
    g_msk = sbt("g_msk", [P, 8], F32)
    g_neg = sbt("g_neg", [P, 8], F32)
    g_exp = sbt("g_exp", [P, 8], F32)
    g_sc1 = sbt("g_sc1", [P, 1], F32)
    g_sc2 = sbt("g_sc2", [P, 1], F32)
    g_sc3 = sbt("g_sc3", [P, 1], F32)
    sq = sbt("sq", [P, 1], F32)
    rmst = sbt("rmst", [P, 1], F32)
    rrms = sbt("rrms", [P, 1], F32)
    rec1 = sbt("rec1", [P, 32], F32)
    rec2 = sbt("rec2", [P, 32], F32)
    c_epsln = sbt("c_epsln", [P, 1], F32)
    c_epsrms = sbt("c_epsrms", [P, 1], F32)
    c_invd = sbt("c_invd", [P, 1], F32)

    kS = comb[:, 0: NT * 512]
    vS = comb[:, NT * 512: 2 * NT * 512]

    pA = pst("pA", [P, 512], F32)
    pB = pst("pB", [P, 512], F32)
    pC = pst("pC", [P, 512], F32)
    pD = pst("pD", [P, 512], F32)
    pE = pst("pE", [P, 1024], BF16)
    pF = pst("pF", [P, 1024], BF16)
    pG = pst("pG", [P, 512], F32)
    pH = pst("pH", [P, 512], F32)

    s_x = Sem(es.enter_context(nc.semaphore("s_x")))
    s_xa = [Sem(es.enter_context(nc.semaphore("s_xa%d" % i))) for i in range(2)]
    s_wb = [Sem(es.enter_context(nc.semaphore("s_wb%d" % i))) for i in range(2)]
    s_row = [Sem(es.enter_context(nc.semaphore("s_row%d" % i))) for i in range(2)]
    s_sl = [Sem(es.enter_context(nc.semaphore("s_sl%d" % i))) for i in range(2)]
    s_const = Sem(es.enter_context(nc.semaphore("s_const")))
    s_bnc = Sem(es.enter_context(nc.semaphore("s_bnc")))
    s_pe = Sem(es.enter_context(nc.semaphore("s_pe")))
    s_ac = Sem(es.enter_context(nc.semaphore("s_ac")))
    s_ve = Sem(es.enter_context(nc.semaphore("s_ve")))
    s_gp = Sem(es.enter_context(nc.semaphore("s_gp")))
    s_sta = [Sem(es.enter_context(nc.semaphore("s_sta%d" % i))) for i in range(2)]
    s_xp2 = [Sem(es.enter_context(nc.semaphore("s_xp%d" % i))) for i in range(2)]
    s_xpg = [Sem(es.enter_context(nc.semaphore("s_xpg%d" % i))) for i in range(2)]
    s_bw = {(hb, t): Sem(es.enter_context(nc.semaphore("s_bw%d%d" % (hb, t))))
            for hb in range(2) for t in range(NT)}
    s_tr = {(hb, th): Sem(es.enter_context(nc.semaphore("s_tr%d%d" % (hb, th))))
            for hb in range(2) for th in range(2)}

    sched = []

    def step(eng, waits, fn):
        waits = tuple((s.h, v) for (s, v) in waits if v > 0)

        def run(e, fn=fn, waits=waits):
            for (h, v) in waits:
                e.wait_ge(h, v)
            fn(e)
        sched.append((eng, run))

    def dma(eng, sem, out_ap, in_ap, waits=(), **kw):
        c = sem.bump(16)

        def fn(e, out_ap=out_ap, in_ap=in_ap, kw=kw, sem=sem):
            e.dma_start(out=out_ap, in_=in_ap, **kw).then_inc(sem.h, 16)
        step(eng, waits, fn)
        return c

    step('gp', (), lambda e: e.memset(ones_r[:, :], 1.0).then_inc(s_gp.h, 1))
    s_gp.bump()
    step('gp', (), lambda e: e.memset(ones_c[:, :], 1.0).then_inc(s_gp.h, 1))
    s_gp.bump()
    step('gp', (), lambda e: e.memset(kvq[:, :], 0.0).then_inc(s_gp.h, 1))
    s_gp.bump()
    step('gp', (), lambda e: e.memset(c_epsln[:, :], EPS_LN).then_inc(s_gp.h, 1))
    s_gp.bump()
    step('gp', (), lambda e: e.memset(c_epsrms[:, :], EPS_RMS).then_inc(s_gp.h, 1))
    s_gp.bump()
    step('gp', (), lambda e: e.memset(c_invd[:, :], 1.0 / D).then_inc(s_gp.h, 1))
    s_gp.bump()
    gp_const = s_gp.n
    x_ident = dma('sp', s_x, id_sb[:, :], ident.ap())
    dma('gp', s_const, rW_sb[:, :], rWr.ap())
    dma('gp', s_const, rb_sb[:, :], rbias.ap())
    nwv = dma('gp', s_row[1], row_b[:, :], nwg.ap())
    w_const = s_const.n

    ln_ts_done = {}
    bnc_half = {}
    xnT0_last_use = [0]

    xn_p = [comb[:, 12288 + i * D: 12288 + (i + 1) * D] for i in range(2)]

    ln_x_free = {}

    def ln_tile(hb, t, xbuf, xnbuf, xv, copy_extra=(), xsem=None):
        bsth, mvh, r1h, r2h = bst[hb], mv[hb], rstd1[hb], rstd2[hb]

        def bn(e, xbuf=xbuf, bsth=bsth):
            for i in range(4):
                ins = e.bn_stats(bsth[:, 6 * i:6 * i + 6],
                                 xbuf[:, 512 * i:512 * (i + 1)])
            ins.then_inc(s_ve.h, 1)
        step('ve', [(xsem, xv)], bn)
        bnv = s_ve.bump()
        step('ve', [(s_ve, bnv)],
             lambda e, mvh=mvh, bsth=bsth: e.bn_aggr(mvh[:, :], bsth[:, :]).then_inc(s_ve.h, 1))
        agv = s_ve.bump()
        step('ac', [(s_ve, agv), (s_gp, gp_const)],
             lambda e, mvh=mvh, r1h=r1h: e.activation(r1h[:, :], mvh[:, 1:2], AF.Sqrt,
                                                      bias=c_epsln[:, :]).then_inc(s_ac.h, 1))
        acv = s_ac.bump()
        step('ve', [(s_ac, acv)],
             lambda e, r1h=r1h, r2h=r2h: e.reciprocal(r2h[:, :], r1h[:, :]).then_inc(s_ve.h, 1))
        rv = s_ve.bump()

        def xnts(e, xbuf=xbuf, xnbuf=xnbuf, mvh=mvh, r2h=r2h):
            e.tensor_scalar(xnbuf, xbuf[:, :], mvh[:, 0:1],
                            r2h[:, :], ALU.subtract, ALU.mult).then_inc(s_ve.h, 1)
        step('ve', [(s_ve, rv)] + list(copy_extra), xnts)
        ln_ts_done[(hb, t)] = s_ve.bump()
        ln_x_free[(hb, t)] = ln_ts_done[(hb, t)]

    def emit_treads(hb, dstT, th, queue, c0, c1):
        ww = [(s_bw[(hb, 4 * th + i)], 16) for i in range(4)]
        for c in range(c0, c1):
            dma(queue, s_tr[(hb, th)],
                dstT[:, c * TL + th * 512: c * TL + (th + 1) * 512],
                xn_dram[hb * TL + th * 512: hb * TL + (th + 1) * 512,
                        c * P:(c + 1) * P],
                waits=ww, transpose=True)
        bnc_half[(hb, th)] = (s_tr[(hb, th)], 16 * NCH)

    for t in range(NT):
        w = [(s_ve, ln_x_free[(1, t - 2)])] if t >= 2 else []
        xv = dma('sp', s_xa[t % 2], x_t[t % 2][:, :], xa[t * P:(t + 1) * P, :],
                 waits=w)
        ce = [(s_bw[(1, t - 2)], 16)] if t >= 2 else []
        ln_tile(1, t, x_t[t % 2], xn_t[t % 2][:, :], xv, copy_extra=ce,
                xsem=s_xa[t % 2])
        dma('ac', s_bw[(1, t)],
            xn_dram[TL + t * P: TL + (t + 1) * P, :],
            xn_t[t % 2][:, :], waits=[(s_ve, s_ve.n)])
    xp_loads = {}
    for t in range(2):
        xp_loads[t] = dma('gp', s_xpg[t], fo_t[t][:, :], xb[t * P:(t + 1) * P, :])
    emit_treads(1, xnT, 0, 'sp', 0, 16)
    emit_treads(1, xnT, 1, 'sp', 0, 16)
    for t in range(NT):
        ce = [(s_bw[(0, t - 2)], 16)] if t >= 2 else []
        ln_tile(0, t, fo_t[t % 2], xn_p[t % 2], xp_loads[t], copy_extra=ce,
                xsem=(s_xpg[t] if t < 2 else s_xp2[t % 2]))
        if t + 2 < NT:
            xp_loads[t + 2] = dma('ac', s_xp2[t % 2], fo_t[t % 2][:, :],
                                  xb[(t + 2) * P:(t + 3) * P, :],
                                  waits=[(s_ve, ln_x_free[(0, t)])])
        dma('sp', s_bw[(0, t)],
            xn_dram[t * P:(t + 1) * P, :],
            xn_p[t % 2], waits=[(s_ve, s_ve.n)])
    emit_treads(0, qT, 0, 'sp', 0, 16)
    emit_treads(0, qT, 1, 'sp', 0, 16)

    wbuf_last_use = [0, 0]
    kvks_last_mm = [0]
    kvks_group_evac = {}
    kvbuf = {("k", 1): comb[:, 0:4096],
             ("v", 1): comb[:, 4096:8192],
             ("k", 0): comb[:, 8192:12288],
             ("v", 0): comb[:, 12288:16384]}
    srcT_of = {1: xnT, 0: qT}

    def phase_kv():
        evac_counts = []
        for s in range(NS):
            wvk = dma('gp', s_wb[0], wb[0][:, :], WkR[s],
                      waits=[(s_pe, wbuf_last_use[0])])
            wvv = dma('gp', s_wb[1], wb[1][:, :], WvR[s],
                      waits=[(s_pe, wbuf_last_use[1])])
            for hb in (1, 0):
                for wi, which in enumerate(("k", "v")):
                    bi = wi
                    wv = wvk if which == "k" else wvv
                    for t in range(NT):
                        gi = len(evac_counts)
                        pX = (pA, pB, pG, pH)[gi % 4]
                        waits = []
                        if t == 0:
                            waits.append((s_wb[wi], wv))
                        if t % 4 == 0:
                            waits.append(bnc_half[(hb, t // 4)])
                        if gi >= 4:
                            waits.append((s_ac, evac_counts[-4]))

                        def grp(e, t=t, bi=bi, pX=pX, srcT=srcT_of[hb]):
                            for c in range(NCH):
                                ins = e.matmul(
                                    pX[:, :],
                                    srcT[:, c * TL + t * P: c * TL + (t + 1) * P],
                                    wb[bi][:, c * 512:(c + 1) * 512],
                                    start=(c == 0), stop=(c == NCH - 1))
                            ins.then_inc(s_pe.h, 1)
                        step('pe', waits, grp)
                        gv = s_pe.bump()
                        wbuf_last_use[bi] = gv
                        dst = kvbuf[(which, hb)][:, t * 512:(t + 1) * 512]
                        func = AF.Relu if which == "k" else AF.Copy
                        evw = [(s_pe, gv)]
                        if kvks_last_mm[0] > 0:
                            evw.append((s_pe, kvks_last_mm[0]))
                        if which == "v" and hb == 0:
                            evw.extend((s_bw[(0, i)], 16) for i in range(NT))

                        def ev(e, dst=dst, func=func, pX=pX):
                            e.activation(dst, pX[:, :], func).then_inc(s_ac.h, 1)
                        step('ac', evw, ev)
                        evac_counts.append(s_ac.bump())
            slice_evacs = s_ac.n
            for half in range(2):
                pX = pC if half == 0 else pD
                waits = [(s_ac, slice_evacs)] if half == 0 else []
                if s == 0 and half == 0:
                    waits.append((s_gp, gp_const))
                if s > 0:
                    waits.append((s_ve, kvks_group_evac[(s - 1, half)]))

                def kvg(e, s=s, half=half, pX=pX):
                    first_mm = True
                    for ci in range(2):
                        off = ci * 258
                        cc = 2 * half + ci
                        for hb in (1, 0):
                            kH = kvbuf[("k", hb)]
                            vH = kvbuf[("v", hb)]
                            for t in range(NT):
                                e.matmul(pX[:, off:off + 128],
                                         kH[:, t * 512 + cc * 128: t * 512 + (cc + 1) * 128],
                                         vH[:, t * 512 + cc * 128: t * 512 + (cc + 1) * 128],
                                         start=first_mm, stop=False)
                                first_mm = False
                                ins = e.matmul(
                                    pX[:, off + 128:off + 129],
                                    kH[:, t * 512 + cc * 128: t * 512 + (cc + 1) * 128],
                                    ones_c[:, :],
                                    start=False,
                                    stop=(ci == 1 and hb == 0 and t == NT - 1))
                    ins.then_inc(s_pe.h, 1)
                step('pe', waits, kvg)
                gv = s_pe.bump()
                kvks_last_mm[0] = gv

                def kve(e, s=s, half=half, pX=pX):
                    for ci in range(2):
                        c = 4 * s + 2 * half + ci
                        ins = e.tensor_copy(kvks[:, c * 129:(c + 1) * 129],
                                            pX[:, ci * 258: ci * 258 + 129])
                    ins.then_inc(s_ve.h, 1)
                step('ve', [(s_pe, gv)], kve)
                kvks_group_evac[(s, half)] = s_ve.bump()

    phase_kv()
    kv1_ac_done = s_ac.n
    kvks_all_done = s_ve.n

    def kvq_build(e):
        for c in range(NCH):
            c0 = c * 130
            k0 = c * 129
            e.tensor_copy(kvq[0:64, c0:c0 + 64], kvks[0:64, k0:k0 + 64])
            e.tensor_copy(kvq[64:128, c0 + 64:c0 + 128], kvks[64:128, k0 + 64:k0 + 128])
            e.tensor_copy(kvq[0:64, c0 + 128:c0 + 129], kvks[0:64, k0 + 128:k0 + 129])
            ins = e.tensor_copy(kvq[64:128, c0 + 129:c0 + 130],
                                kvks[64:128, k0 + 128:k0 + 129])
        ins.then_inc(s_ve.h, 1)
    step('ve', [(s_ve, kvks_all_done), (s_gp, gp_const)], kvq_build)
    kvq_done = s_ve.bump()

    q_evacs = []
    for jj in range(4):
        bi = jj % 2
        for j4 in range(4):
            wv = dma('gp', s_wb[bi], wb[bi][:, j4 * D:(j4 + 1) * D], WqR[jj * 4 + j4],
                     waits=[(s_pe, wbuf_last_use[bi])] if j4 == 0 else ())
        for j4 in range(4):
            j = jj * 4 + j4
            for th in range(2):
                gi = len(q_evacs)
                pX = pA if gi % 2 == 0 else pB
                waits = [(s_wb[bi], wv)] if (j4 == 0 and th == 0) else []
                waits.append((s_ac, q_evacs[-2] if gi >= 2 else kv1_ac_done))

                def qg(e, j4=j4, th=th, bi=bi, pX=pX):
                    for c in range(NCH):
                        ins = e.matmul(
                            pX[:, :],
                            wb[bi][:, j4 * D + c * P: j4 * D + (c + 1) * P],
                            xnT[:, c * TL + th * 512: c * TL + (th + 1) * 512],
                            start=(c == 0), stop=(c == NCH - 1))
                    ins.then_inc(s_pe.h, 1)
                step('pe', waits, qg)
                gv = s_pe.bump()
                wbuf_last_use[bi] = gv

                def qe(e, j=j, th=th, pX=pX):
                    e.activation(qT[:, j * TL + th * 512: j * TL + (th + 1) * 512],
                                 pX[:, :], AF.Relu).then_inc(s_ac.h, 1)
                step('ac', [(s_pe, gv)], qe)
                q_evacs.append(s_ac.bump())
    q_done = s_ac.n

    for t in range(NT):
        def rg(e, t=t):
            e.matmul(pG[:, t * 8:(t + 1) * 8], ones_r[:, :], rb_sb[:, :],
                     start=True, stop=False)
            for c in range(NCH):
                ins = e.matmul(pG[:, t * 8:(t + 1) * 8],
                               xnT[:, c * TL + t * P: c * TL + (t + 1) * P],
                               rW_sb[:, c * 8:(c + 1) * 8],
                               start=False, stop=(c == NCH - 1))
            ins.then_inc(s_pe.h, 1)
        step('pe', [(s_const, w_const), (s_ac, kv1_ac_done), (s_gp, gp_const)], rg)
        s_pe.bump()
    rt_pe_done = s_pe.n

    nwb_evac = 0
    for r in range(4):
        waits = [(s_row[1], nwv), (s_gp, gp_const), (s_ac, kv1_ac_done)]
        if r > 0:
            waits.append((s_ve, nwb_evac))

        def nb(e, r=r):
            e.matmul(pH[:, :], ones_r[:, :], row_b[:, r * 512:(r + 1) * 512],
                     start=True, stop=True).then_inc(s_pe.h, 1)
        step('pe', waits, nb)
        gv = s_pe.bump()

        def nbe(e, r=r):
            e.tensor_copy(nwb[:, r * 512:(r + 1) * 512], pH[:, :]).then_inc(s_ve.h, 1)
        step('ve', [(s_pe, gv)], nbe)
        nwb_evac = s_ve.bump()
    nwb_done = nwb_evac
    nwb_pe_last = s_pe.n

    for t in range(NT):
        gcol = gates[:, t * 8:(t + 1) * 8]
        l = pG[:, t * 8:(t + 1) * 8]

        def chain(emitfn, waits=()):
            step('ve', list(waits) + [(s_ve, s_ve.n)],
                 lambda e, f=emitfn: f(e).then_inc(s_ve.h, 1))
            return s_ve.bump()

        chain(lambda e, l=l: e.reduce_max(g_sc1[:, :], l, axis=AX, negate=True),
              waits=[(s_pe, rt_pe_done)])
        mskv = chain(lambda e, l=l: e.tensor_scalar(g_msk[:, :], l, g_sc1[:, :],
                                                    None, ALU.add))
        step('ac', [(s_ve, mskv)],
             lambda e: e.activation(g_exp[:, :], g_msk[:, :], AF.Exp).then_inc(s_ac.h, 1))
        expv = s_ac.bump()
        chain(lambda e: e.tensor_scalar(g_neg[:, :], g_msk[:, :], 0.0, None, ALU.is_ge))
        chain(lambda e: e.scalar_tensor_tensor(g_neg[:, :], g_neg[:, :], -BIG,
                                               g_msk[:, :], ALU.mult, ALU.add))
        chain(lambda e: e.reduce_max(g_sc2[:, :], g_neg[:, :], axis=AX))
        chain(lambda e: e.tensor_scalar(g_neg[:, :], g_msk[:, :], g_sc2[:, :],
                                        None, ALU.is_ge))
        chain(lambda e, gcol=gcol: e.tensor_tensor(gcol, g_neg[:, :], g_exp[:, :],
                                                   ALU.mult),
              waits=[(s_ac, expv)])
        chain(lambda e, gcol=gcol: e.reduce_sum(g_sc1[:, :], gcol, axis=AX))
        chain(lambda e: e.reciprocal(g_sc3[:, :], g_sc1[:, :]))
        chain(lambda e, gcol=gcol: e.tensor_scalar(gcol, gcol, g_sc3[:, :],
                                                   None, ALU.mult))
    gates_done = s_ve.n

    div_done = []
    rec_done = {}
    norm_t_done = {}
    at_gi = 0
    for t in range(NT):
        pN = pC if t % 2 == 0 else pG
        nw_waits = ([(s_ve, kvq_done), (s_ac, q_done), (s_ve, gates_done)]
                    if t == 0 else [])
        if t >= 2:
            nw_waits.append((s_ve, rec_done[t - 2]))

        def ng(e, t=t, pN=pN):
            for c in range(NCH):
                ins = e.matmul(pN[:, c * 2: c * 2 + 2],
                               qT[:, c * TL + t * P: c * TL + (t + 1) * P],
                               kvq[:, c * 130 + 128: c * 130 + 130],
                               start=True, stop=True)
            ins.then_inc(s_pe.h, 1)
        step('pe', nw_waits, ng)
        ngv = s_pe.bump()
        step('ve', [(s_pe, ngv), (s_ve, s_ve.n)],
             lambda e, pN=pN: e.tensor_scalar(rec1[:, :], pN[:, 0:32], EPS_CLIP,
                                              None, ALU.max).then_inc(s_ve.h, 1))
        c0 = s_ve.bump()
        step('ve', [(s_ve, c0)],
             lambda e: e.reciprocal(rec2[:, :], rec1[:, :]).then_inc(s_ve.h, 1))
        rec_done[t] = s_ve.bump()
        for cg in range(4):
            pX = (pD, pH, pA, pB)[at_gi % 4]
            waits = [(s_ve, kvq_done)] if at_gi == 0 else []
            if at_gi >= 4:
                waits.append((s_ve, div_done[-4]))
            elif at_gi == 1:
                waits.append((s_ve, nwb_done))
            elif at_gi in (2, 3):
                waits.append((s_ac, q_done))

            def ag(e, t=t, cg=cg, pX=pX):
                for ci in range(4):
                    c = cg * 4 + ci
                    ins = e.matmul(pX[:, ci * P:(ci + 1) * P],
                                   qT[:, c * TL + t * P: c * TL + (t + 1) * P],
                                   kvq[:, c * 130: c * 130 + 128],
                                   start=True, stop=True)
                ins.then_inc(s_pe.h, 1)
            step('pe', waits, ag)
            gv = s_pe.bump()

            def dv(e, t=t, cg=cg, pX=pX):
                for ci in range(4):
                    c = cg * 4 + ci
                    for h in range(2):
                        ins = e.tensor_scalar(
                            qT[:, c * TL + t * P + h * 64: c * TL + t * P + (h + 1) * 64],
                            pX[:, ci * P + h * 64: ci * P + (h + 1) * 64],
                            rec2[:, c * 2 + h: c * 2 + h + 1], None, ALU.mult)
                ins.then_inc(s_ve.h, 1)
            step('ve', [(s_pe, gv), (s_ve, rec_done[t])], dv)
            div_done.append(s_ve.bump())
            at_gi += 1
        norm_t_done[t] = s_ve.n

    eb_last_use = [0, nwb_pe_last]
    comb_done = {}
    ex_silu = []
    ex_stt = []
    ex_gi = 0
    for e_i in range(NE):
        erow = row_a if e_i % 2 == 0 else row_b
        ebv = dma('gp', s_row[e_i % 2], erow[:, :], ebias[e_i:e_i + 1, :],
                  waits=[(s_pe, eb_last_use[e_i % 2])])
        for s in range(NS):
            bi = (e_i * NS + s) % 2
            wv = dma('gp', s_wb[bi], wb[bi][:, :], WeR[e_i, s],
                     waits=[(s_pe, wbuf_last_use[bi])])
            for t in range(NT):
                pX = pA if ex_gi % 2 == 0 else pB
                waits = ([(s_wb[bi], wv), (s_row[e_i % 2], ebv)]
                         if t == 0 else [])
                if ex_gi >= 2:
                    waits.append((s_ac, ex_silu[-2]))
                else:
                    waits.append((s_ac, q_done))
                    waits.append((s_ve, div_done[-1]))

                def eg(e, s=s, t=t, bi=bi, pX=pX, erow=erow):
                    e.matmul(pX[:, :], ones_r[:, :], erow[:, s * 512:(s + 1) * 512],
                             start=True, stop=False)
                    for c in range(NCH):
                        ins = e.matmul(
                            pX[:, :],
                            xnT[:, c * TL + t * P: c * TL + (t + 1) * P],
                            wb[bi][:, c * 512:(c + 1) * 512],
                            start=False, stop=(c == NCH - 1))
                    ins.then_inc(s_pe.h, 1)
                step('pe', waits, eg)
                gv = s_pe.bump()
                wbuf_last_use[bi] = gv
                eb_last_use[e_i % 2] = gv
                sw = [(s_pe, gv)]
                if ex_gi >= 2:
                    sw.append((s_ve, ex_stt[-2]))

                def sl(e, ex_gi=ex_gi, pX=pX):
                    e.activation(silu_t[ex_gi % 2][:, :], pX[:, :],
                                 AF.Silu).then_inc(s_ac.h, 1)
                step('ac', sw, sl)
                av = s_ac.bump()
                ex_silu.append(av)
                dst = comb[:, t * D + s * 512: t * D + (s + 1) * 512]
                gcolap = gates[:, t * 8 + e_i: t * 8 + e_i + 1]

                def cb(e, e_i=e_i, dst=dst, gcolap=gcolap, ex_gi=ex_gi):
                    srcb = silu_t[ex_gi % 2][:, :]
                    if e_i == 0:
                        ins = e.tensor_scalar(dst, srcb, gcolap, None, ALU.mult)
                    else:
                        ins = e.scalar_tensor_tensor(dst, srcb, gcolap, dst,
                                                     ALU.mult, ALU.add)
                    ins.then_inc(s_ve.h, 1)
                step('ve', [(s_ac, av), (s_ve, gates_done)], cb)
                ex_stt.append(s_ve.bump())
                if e_i == NE - 1 and s == NS - 1:
                    comb_done[t] = s_ve.n
                ex_gi += 1
    ex_ac_done = s_ac.n

    xnT_view = xnT[:, :].rearrange("p (c x) -> p c x", c=NCH)

    def emit_transpose_phase(src_done, slicer, prev_pe_pf=0):
        evacs = []
        done = {}
        ti = 0
        for t in range(NT):
            for ch in range(2):
                pX = pE if ti % 2 == 0 else pF
                waits = [(s_ve, src_done[t]), (s_x, x_ident)]
                waits.append((s_ve, evacs[-2] if ti >= 2 else prev_pe_pf))

                def tg(e, t=t, ch=ch, pX=pX):
                    for ci in range(8):
                        c = ch * 8 + ci
                        ins = e.transpose(pX[:, ci * P:(ci + 1) * P],
                                          slicer(t, c),
                                          id_sb[:, :])
                    ins.then_inc(s_pe.h, 1)
                step('pe', waits, tg)
                gv = s_pe.bump()

                def te(e, t=t, ch=ch, pX=pX):
                    dstv = xnT_view[:, ch * 8:(ch + 1) * 8, t * P:(t + 1) * P]
                    srcv = pX[:, :].rearrange("p (c x) -> p c x", c=8)
                    e.tensor_copy(dstv, srcv).then_inc(s_ve.h, 1)
                step('ve', [(s_pe, gv)], te)
                evacs.append(s_ve.bump())
                ti += 1
            done[t] = s_ve.n
        return done

    tc_done = emit_transpose_phase(
        comb_done, lambda t, c: comb[:, t * D + c * P: t * D + (c + 1) * P])

    pjbv = dma('gp', s_row[0], row_a[:, :], pjbias.ap(), waits=[(s_pe, eb_last_use[0])])
    pj_y = []
    restream = {}
    pj_gi = 0
    for s in range(NS):
        bi = s % 2
        wv = dma('gp', s_wb[bi], wb[bi][:, :], PjR[s], waits=[(s_pe, wbuf_last_use[bi])])
        for t in range(NT):
            pX = pA if pj_gi % 2 == 0 else pB
            waits = ([(s_wb[bi], wv), (s_row[0], pjbv)] if t == 0 else [])
            waits.append((s_ve, tc_done[t]))
            if pj_gi >= 2:
                waits.append((s_ve, pj_y[-2]))
            else:
                waits.append((s_ac, ex_ac_done))

            def pg(e, s=s, t=t, bi=bi, pX=pX):
                e.matmul(pX[:, :], ones_r[:, :], row_a[:, s * 512:(s + 1) * 512],
                         start=True, stop=False)
                for c in range(NCH):
                    ins = e.matmul(
                        pX[:, :],
                        xnT[:, c * TL + t * P: c * TL + (t + 1) * P],
                        wb[bi][:, c * 512:(c + 1) * 512],
                        start=False, stop=(c == NCH - 1))
                ins.then_inc(s_pe.h, 1)
            step('pe', waits, pg)
            gv = s_pe.bump()
            wbuf_last_use[bi] = gv
            xw = [(s_ve, pj_y[-2])] if len(pj_y) >= 2 else []
            xv = dma('sp', s_sl[pj_gi % 2], xn_sl[pj_gi % 2][:, :],
                     xn_dram[TL + t * P: TL + (t + 1) * P, s * 512:(s + 1) * 512],
                     waits=xw)
            dst = comb[:, t * D + s * 512: t * D + (s + 1) * 512]

            def ye(e, dst=dst, pX=pX, pj_gi=pj_gi):
                e.tensor_tensor(dst, pX[:, :], xn_sl[pj_gi % 2][:, :],
                                ALU.add).then_inc(s_ve.h, 1)
            step('ve', [(s_pe, gv), (s_sl[pj_gi % 2], xv)], ye)
            pj_y.append(s_ve.bump())
            pj_gi += 1
            if s == NS - 1:
                yv = s_ve.n
                xw = [(s_ve, restream[t - 2])] if t >= 2 else \
                     [(s_ve, ln_ts_done[(1, NT - 2 + t)])]
                xv2 = dma('sp', s_xa[t % 2], x_t[t % 2][:, :],
                          xa[t * P:(t + 1) * P, :], waits=xw)
                sqw = [(s_ve, yv)]
                if t >= 2:
                    sqw.append((s_ve, restream[t - 2]))

                def sqf(e, t=t):
                    e.activation(fo_t[t % 2][:, :], comb[:, t * D:(t + 1) * D],
                                 AF.Square, accum_out=sq[:, :]).then_inc(s_ac.h, 1)
                step('ac', sqw, sqf)
                v1 = s_ac.bump()
                step('ac', [(s_ac, v1)],
                     lambda e: e.activation(rmst[:, :], sq[:, :], AF.Sqrt,
                                            bias=c_epsrms[:, :],
                                            scale=c_invd[:, :]).then_inc(s_ac.h, 1))
                v2 = s_ac.bump()
                step('ve', [(s_ac, v2)],
                     lambda e: e.reciprocal(rrms[:, :], rmst[:, :]).then_inc(s_ve.h, 1))
                v3 = s_ve.bump()

                def fstt(e, t=t):
                    e.scalar_tensor_tensor(fo_t[t % 2][:, :], comb[:, t * D:(t + 1) * D],
                                           rrms[:, :], nwb[:, :], ALU.mult,
                                           ALU.mult).then_inc(s_ve.h, 1)
                step('ve', [(s_ve, v3), (s_ve, nwb_done)], fstt)
                v4 = s_ve.bump()

                def fadd(e, t=t):
                    e.tensor_tensor(comb[:, t * D:(t + 1) * D], fo_t[t % 2][:, :],
                                    x_t[t % 2][:, :], ALU.add).then_inc(s_ve.h, 1)
                step('ve', [(s_ve, v4), (s_xa[t % 2], xv2)], fadd)
                restream[t] = s_ve.bump()
    pj_pe_done = s_pe.n
    pj_act_done = s_ac.n
    pj_ve_done = s_ve.n

    bov = dma('gp', s_row[1], row_b[:, :], obias.ap(), waits=[(s_pe, eb_last_use[1])])

    am_done = emit_transpose_phase(
        norm_t_done, lambda t, c: qT[:, c * TL + t * P: c * TL + (t + 1) * P],
        prev_pe_pf=tc_done[NT - 1])

    wo_evacs = []
    wo_gi = 0
    for s in range(NS):
        bi = s % 2
        wv = dma('gp', s_wb[bi], wb[bi][:, :], WoR[s], waits=[(s_pe, wbuf_last_use[bi])])
        for t in range(NT):
            pX = pA if wo_gi % 2 == 0 else pB
            waits = ([(s_wb[bi], wv), (s_row[1], bov)] if t == 0 else [])
            waits.append((s_ve, am_done[t]))
            waits.append((s_ve, wo_evacs[-2] if wo_gi >= 2 else pj_ve_done))

            def wg(e, s=s, t=t, bi=bi, pX=pX):
                e.matmul(pX[:, :], ones_r[:, :], row_b[:, s * 512:(s + 1) * 512],
                         start=True, stop=False)
                for c in range(NCH):
                    ins = e.matmul(
                        pX[:, :],
                        xnT[:, c * TL + t * P: c * TL + (t + 1) * P],
                        wb[bi][:, c * 512:(c + 1) * 512],
                        start=False, stop=(c == NCH - 1))
                ins.then_inc(s_pe.h, 1)
            step('pe', waits, wg)
            gv = s_pe.bump()
            wbuf_last_use[bi] = gv
            sl_i = wo_gi % 2
            ew = [(s_pe, gv)]
            if wo_gi >= 2:
                ew.append((s_sta[sl_i], 16 * (wo_gi // 2)))

            def we(e, sl_i=sl_i, pX=pX, t=t, s=s):
                e.tensor_tensor(att_sl[sl_i][:, :], pX[:, :],
                                comb[:, t * D + s * 512: t * D + (s + 1) * 512],
                                ALU.add).then_inc(s_ve.h, 1)
            step('ve', ew, we)
            wo_evacs.append(s_ve.bump())
            dma('sp', s_sta[sl_i], out[t * P:(t + 1) * P, s * 512:(s + 1) * 512],
                att_sl[sl_i][:, :], waits=[(s_ve, s_ve.n)])
            wo_gi += 1

    step('sp', [(sm, sm.n) for sm in s_sta], lambda e: e.nop())

    with nc.Block() as block:
        @block.tensor
        def _(e):
            for k, fn in sched:
                if k == 'pe':
                    fn(e)

        @block.vector
        def _(e):
            for k, fn in sched:
                if k == 've':
                    fn(e)

        @block.scalar
        def _(e):
            for k, fn in sched:
                if k == 'ac':
                    fn(e)

        @block.gpsimd
        def _(e):
            for k, fn in sched:
                if k == 'gp':
                    fn(e)

        @block.sync
        def _(e):
            for k, fn in sched:
                if k == 'sp':
                    fn(e)

    es.close()
    return nc


_NC_CACHE = [None]


def kernel(L0_attn_hidden_states, L0_attn_Wq, L0_attn_Wk, L0_attn_Wv,
           L0_attn_Wo, L0_attn_bo, L0_ffn_router_w, L0_ffn_router_b,
           L0_ffn_expert_w, L0_ffn_expert_b, L0_ffn_proj_w, L0_ffn_proj_b,
           L0_ffn_norm_w):
    x = np.asarray(L0_attn_hidden_states, dtype=np.float32)
    B, S, _ = x.shape

    def mov(w):
        return np.ascontiguousarray(
            np.asarray(w, np.float32).reshape(NCH, P, NS, 512).transpose(2, 1, 0, 3)
        ).reshape(NS, P, NCH * 512)

    WqR = np.ascontiguousarray(
        np.asarray(L0_attn_Wq, np.float32).reshape(NCH, P, NCH, P).transpose(2, 1, 0, 3)
    ).reshape(NCH, P, D)
    shared = dict(
        WqR=WqR,
        WkR=mov(L0_attn_Wk),
        WvR=mov(L0_attn_Wv),
        WoR=mov(L0_attn_Wo),
        PjR=mov(L0_ffn_proj_w),
        WeR=np.stack([mov(np.asarray(L0_ffn_expert_w, np.float32)[e])
                      for e in range(NE)]),
        rWr=np.ascontiguousarray(
            np.asarray(L0_ffn_router_w, np.float32).reshape(NCH, P, 8).transpose(1, 0, 2)
        ).reshape(P, NCH * 8),
        rbias=np.asarray(L0_ffn_router_b, np.float32).reshape(1, 8),
        ebias=np.ascontiguousarray(np.asarray(L0_ffn_expert_b, np.float32)),
        pjbias=np.asarray(L0_ffn_proj_b, np.float32).reshape(1, D),
        obias=np.asarray(L0_attn_bo, np.float32).reshape(1, D),
        nwg=np.asarray(L0_ffn_norm_w, np.float32).reshape(1, D),
        ident=np.eye(P, dtype=ml_dtypes.bfloat16).view(np.uint16),
    )

    in_maps = []
    for core in range(8):
        b = core // 2
        half = core % 2
        in_maps.append(dict(
            xa=np.ascontiguousarray(x[b, half * TL:(half + 1) * TL, :]),
            xb=np.ascontiguousarray(x[b, (1 - half) * TL:(2 - half) * TL, :]),
            **shared))

    if _NC_CACHE[0] is None:
        _NC_CACHE[0] = build_nc()
    res = run_bass_kernel_spmd(_NC_CACHE[0], in_maps, core_ids=list(range(8)))

    outp = np.empty((B, S, D), dtype=np.float32)
    for core in range(8):
        b = core // 2
        half = core % 2
        outp[b, half * TL:(half + 1) * TL, :] = res.results[core]["out"]
    return outp


# version 26
# speedup vs baseline: 1.0053x; 1.0053x over previous
import numpy as np
import ml_dtypes
import concourse.bass as bass
import concourse.mybir as mybir
from concourse.bass_utils import run_bass_kernel_spmd
from contextlib import ExitStack

F32 = mybir.dt.float32
BF16 = mybir.dt.bfloat16
AF = mybir.ActivationFunctionType
ALU = mybir.AluOpType
AX = mybir.AxisListType.X

P = 128
D = 2048
NCH = 16
TL = 1024
NT = 8
NS = 4
NE = 8
EPS_LN = 1e-5
EPS_RMS = 1e-6
EPS_CLIP = 1e-6
BIG = 1e30


class Sem:
    def __init__(self, h):
        self.h = h
        self.n = 0

    def bump(self, k=1):
        self.n += k
        return self.n


def build_nc():
    nc = bass.Bass("TRN2", target_bir_lowering=False, debug=False)

    xa = nc.dram_tensor("xa", [TL, D], F32, kind="ExternalInput")
    xb = nc.dram_tensor("xb", [TL, D], F32, kind="ExternalInput")
    WqR = nc.dram_tensor("WqR", [NCH, P, D], F32, kind="ExternalInput")
    WkR = nc.dram_tensor("WkR", [NS, P, NCH * 512], F32, kind="ExternalInput")
    WvR = nc.dram_tensor("WvR", [NS, P, NCH * 512], F32, kind="ExternalInput")
    WoR = nc.dram_tensor("WoR", [NS, P, NCH * 512], F32, kind="ExternalInput")
    PjR = nc.dram_tensor("PjR", [NS, P, NCH * 512], F32, kind="ExternalInput")
    WeR = nc.dram_tensor("WeR", [NE, NS, P, NCH * 512], F32, kind="ExternalInput")
    rWr = nc.dram_tensor("rWr", [P, NCH * 8], F32, kind="ExternalInput")
    rbias = nc.dram_tensor("rbias", [1, 8], F32, kind="ExternalInput")
    ebias = nc.dram_tensor("ebias", [NE, D], F32, kind="ExternalInput")
    pjbias = nc.dram_tensor("pjbias", [1, D], F32, kind="ExternalInput")
    obias = nc.dram_tensor("obias", [1, D], F32, kind="ExternalInput")
    nwg = nc.dram_tensor("nwg", [1, D], F32, kind="ExternalInput")
    ident = nc.dram_tensor("ident", [P, P], BF16, kind="ExternalInput")
    out = nc.dram_tensor("out", [TL, D], F32, kind="ExternalOutput")

    xn_dram = nc.dram_tensor("xn_dram", [2 * TL, D], BF16)

    es = ExitStack()
    sbt = lambda name, shape, dt: es.enter_context(nc.sbuf_tensor(name, shape, dt))
    pst = lambda name, shape, dt: es.enter_context(nc.psum_tensor(name, shape, dt))

    xnT = sbt("xnT", [P, NCH * TL], BF16)
    qT = sbt("qT", [P, NCH * TL], BF16)
    comb = sbt("comb", [P, NT * D], BF16)
    wb = [sbt("wb%d" % i, [P, NCH * 512], BF16) for i in range(2)]
    x_t = [sbt("x_t%d" % i, [P, D], F32) for i in range(2)]
    fo_t = [sbt("fo_t%d" % i, [P, D], F32) for i in range(2)]
    xn_t = [sbt("xn_t%d" % i, [P, D], BF16) for i in range(2)]
    kvks = sbt("kvks", [P, NCH * 129], BF16)
    kvq = sbt("kvq", [P, NCH * 130], BF16)
    nwb = sbt("nwb", [P, D], BF16)
    rW_sb = sbt("rW_sb", [P, NCH * 8], BF16)
    row_a = sbt("row_a", [1, D], BF16)
    row_b = sbt("row_b", [1, D], BF16)
    rb_sb = sbt("rb_sb", [1, 8], BF16)
    silu_t = [sbt("silu%d" % i, [P, 512], BF16) for i in range(2)]
    xn_sl = [sbt("xnsl%d" % i, [P, 512], BF16) for i in range(2)]
    att_sl = [sbt("attsl%d" % i, [P, 512], F32) for i in range(2)]
    id_sb = sbt("id_sb", [P, P], BF16)
    ones_r = sbt("ones_r", [1, P], BF16)
    ones_c = sbt("ones_c", [P, 1], BF16)
    gates = sbt("gates", [P, NT * 8], F32)
    bst = [sbt("bst%d" % i, [P, 24], F32) for i in range(2)]
    mv = [sbt("mv%d" % i, [P, 2], F32) for i in range(2)]
    rstd1 = [sbt("rstd1_%d" % i, [P, 1], F32) for i in range(2)]
    rstd2 = [sbt("rstd2_%d" % i, [P, 1], F32) for i in range(2)]
    g_msk = sbt("g_msk", [P, 8], F32)
    g_neg = sbt("g_neg", [P, 8], F32)
    g_exp = sbt("g_exp", [P, 8], F32)
    g_sc1 = sbt("g_sc1", [P, 1], F32)
    g_sc2 = sbt("g_sc2", [P, 1], F32)
    g_sc3 = sbt("g_sc3", [P, 1], F32)
    sq = sbt("sq", [P, 1], F32)
    rmst = sbt("rmst", [P, 1], F32)
    rrms = sbt("rrms", [P, 1], F32)
    rec1 = sbt("rec1", [P, 32], F32)
    rec2 = sbt("rec2", [P, 32], F32)
    c_epsln = sbt("c_epsln", [P, 1], F32)
    c_epsrms = sbt("c_epsrms", [P, 1], F32)
    c_invd = sbt("c_invd", [P, 1], F32)

    kS = comb[:, 0: NT * 512]
    vS = comb[:, NT * 512: 2 * NT * 512]

    pA = pst("pA", [P, 512], F32)
    pB = pst("pB", [P, 512], F32)
    pC = pst("pC", [P, 512], F32)
    pD = pst("pD", [P, 512], F32)
    pE = pst("pE", [P, 1024], BF16)
    pF = pst("pF", [P, 1024], BF16)
    pG = pst("pG", [P, 512], F32)
    pH = pst("pH", [P, 512], F32)

    s_x = Sem(es.enter_context(nc.semaphore("s_x")))
    s_xa = [Sem(es.enter_context(nc.semaphore("s_xa%d" % i))) for i in range(2)]
    s_wb = [Sem(es.enter_context(nc.semaphore("s_wb%d" % i))) for i in range(2)]
    s_row = [Sem(es.enter_context(nc.semaphore("s_row%d" % i))) for i in range(2)]
    s_sl = [Sem(es.enter_context(nc.semaphore("s_sl%d" % i))) for i in range(2)]
    s_const = Sem(es.enter_context(nc.semaphore("s_const")))
    s_bnc = Sem(es.enter_context(nc.semaphore("s_bnc")))
    s_pe = Sem(es.enter_context(nc.semaphore("s_pe")))
    s_ac = Sem(es.enter_context(nc.semaphore("s_ac")))
    s_ve = Sem(es.enter_context(nc.semaphore("s_ve")))
    s_gp = Sem(es.enter_context(nc.semaphore("s_gp")))
    s_sta = [Sem(es.enter_context(nc.semaphore("s_sta%d" % i))) for i in range(2)]
    s_xp2 = [Sem(es.enter_context(nc.semaphore("s_xp%d" % i))) for i in range(2)]
    s_xpg = [Sem(es.enter_context(nc.semaphore("s_xpg%d" % i))) for i in range(2)]
    s_bw = {(hb, t): Sem(es.enter_context(nc.semaphore("s_bw%d%d" % (hb, t))))
            for hb in range(2) for t in range(NT)}
    s_tr = {(hb, th): Sem(es.enter_context(nc.semaphore("s_tr%d%d" % (hb, th))))
            for hb in range(2) for th in range(2)}

    sched = []

    def step(eng, waits, fn):
        waits = tuple((s.h, v) for (s, v) in waits if v > 0)

        def run(e, fn=fn, waits=waits):
            for (h, v) in waits:
                e.wait_ge(h, v)
            fn(e)
        sched.append((eng, run))

    def dma(eng, sem, out_ap, in_ap, waits=(), **kw):
        c = sem.bump(16)

        def fn(e, out_ap=out_ap, in_ap=in_ap, kw=kw, sem=sem):
            e.dma_start(out=out_ap, in_=in_ap, **kw).then_inc(sem.h, 16)
        step(eng, waits, fn)
        return c

    step('gp', (), lambda e: e.memset(ones_r[:, :], 1.0).then_inc(s_gp.h, 1))
    s_gp.bump()
    step('gp', (), lambda e: e.memset(ones_c[:, :], 1.0).then_inc(s_gp.h, 1))
    s_gp.bump()
    step('gp', (), lambda e: e.memset(kvq[:, :], 0.0).then_inc(s_gp.h, 1))
    s_gp.bump()
    step('gp', (), lambda e: e.memset(c_epsln[:, :], EPS_LN).then_inc(s_gp.h, 1))
    s_gp.bump()
    step('gp', (), lambda e: e.memset(c_epsrms[:, :], EPS_RMS).then_inc(s_gp.h, 1))
    s_gp.bump()
    step('gp', (), lambda e: e.memset(c_invd[:, :], 1.0 / D).then_inc(s_gp.h, 1))
    s_gp.bump()
    gp_const = s_gp.n
    x_ident = dma('sp', s_x, id_sb[:, :], ident.ap())
    dma('gp', s_const, rW_sb[:, :], rWr.ap())
    dma('gp', s_const, rb_sb[:, :], rbias.ap())
    nwv = dma('gp', s_row[1], row_b[:, :], nwg.ap())
    w_const = s_const.n

    ln_ts_done = {}
    bnc_half = {}
    xnT0_last_use = [0]

    xn_p = [comb[:, 12288 + i * D: 12288 + (i + 1) * D] for i in range(2)]

    ln_x_free = {}

    def ln_tile(hb, t, xbuf, xnbuf, xv, copy_extra=(), xsem=None):
        bsth, mvh, r1h, r2h = bst[hb], mv[hb], rstd1[hb], rstd2[hb]

        def bn(e, xbuf=xbuf, bsth=bsth):
            for i in range(4):
                ins = e.bn_stats(bsth[:, 6 * i:6 * i + 6],
                                 xbuf[:, 512 * i:512 * (i + 1)])
            ins.then_inc(s_ve.h, 1)
        step('ve', [(xsem, xv)], bn)
        bnv = s_ve.bump()
        step('ve', [(s_ve, bnv)],
             lambda e, mvh=mvh, bsth=bsth: e.bn_aggr(mvh[:, :], bsth[:, :]).then_inc(s_ve.h, 1))
        agv = s_ve.bump()
        step('ac', [(s_ve, agv), (s_gp, gp_const)],
             lambda e, mvh=mvh, r1h=r1h: e.activation(r1h[:, :], mvh[:, 1:2], AF.Sqrt,
                                                      bias=c_epsln[:, :]).then_inc(s_ac.h, 1))
        acv = s_ac.bump()
        step('ve', [(s_ac, acv)],
             lambda e, r1h=r1h, r2h=r2h: e.reciprocal(r2h[:, :], r1h[:, :]).then_inc(s_ve.h, 1))
        rv = s_ve.bump()

        def xnts(e, xbuf=xbuf, xnbuf=xnbuf, mvh=mvh, r2h=r2h):
            e.tensor_scalar(xnbuf, xbuf[:, :], mvh[:, 0:1],
                            r2h[:, :], ALU.subtract, ALU.mult).then_inc(s_ve.h, 1)
        step('ve', [(s_ve, rv)] + list(copy_extra), xnts)
        ln_ts_done[(hb, t)] = s_ve.bump()
        ln_x_free[(hb, t)] = ln_ts_done[(hb, t)]

    def emit_treads(hb, dstT, th, queue, c0, c1):
        ww = [(s_bw[(hb, 4 * th + i)], 16) for i in range(4)]
        for c in range(c0, c1):
            dma(queue, s_tr[(hb, th)],
                dstT[:, c * TL + th * 512: c * TL + (th + 1) * 512],
                xn_dram[hb * TL + th * 512: hb * TL + (th + 1) * 512,
                        c * P:(c + 1) * P],
                waits=ww, transpose=True)
        bnc_half[(hb, th)] = (s_tr[(hb, th)], 16 * NCH)

    for t in range(NT):
        w = [(s_ve, ln_x_free[(1, t - 2)])] if t >= 2 else []
        xv = dma('sp', s_xa[t % 2], x_t[t % 2][:, :], xa[t * P:(t + 1) * P, :],
                 waits=w)
        ce = [(s_bw[(1, t - 2)], 16)] if t >= 2 else []
        ln_tile(1, t, x_t[t % 2], xn_t[t % 2][:, :], xv, copy_extra=ce,
                xsem=s_xa[t % 2])
        dma('ac', s_bw[(1, t)],
            xn_dram[TL + t * P: TL + (t + 1) * P, :],
            xn_t[t % 2][:, :], waits=[(s_ve, s_ve.n)])
    xp_loads = {}
    for t in range(2):
        xp_loads[t] = dma('gp', s_xpg[t], fo_t[t][:, :], xb[t * P:(t + 1) * P, :])
    emit_treads(1, xnT, 0, 'sp', 0, 16)
    emit_treads(1, xnT, 1, 'sp', 0, 16)
    for t in range(NT):
        ce = [(s_bw[(0, t - 2)], 16)] if t >= 2 else []
        ln_tile(0, t, fo_t[t % 2], xn_p[t % 2], xp_loads[t], copy_extra=ce,
                xsem=(s_xpg[t] if t < 2 else s_xp2[t % 2]))
        if t + 2 < NT:
            xp_loads[t + 2] = dma('ac', s_xp2[t % 2], fo_t[t % 2][:, :],
                                  xb[(t + 2) * P:(t + 3) * P, :],
                                  waits=[(s_ve, ln_x_free[(0, t)])])
        dma('sp', s_bw[(0, t)],
            xn_dram[t * P:(t + 1) * P, :],
            xn_p[t % 2], waits=[(s_ve, s_ve.n)])
    emit_treads(0, qT, 0, 'sp', 0, 16)
    emit_treads(0, qT, 1, 'sp', 0, 16)

    wbuf_last_use = [0, 0]
    kvks_last_mm = [0]
    kvks_group_evac = {}
    kvbuf = {("k", 1): comb[:, 0:4096],
             ("v", 1): comb[:, 4096:8192],
             ("k", 0): comb[:, 8192:12288],
             ("v", 0): comb[:, 12288:16384]}
    srcT_of = {1: xnT, 0: qT}

    def phase_kv():
        evac_counts = []
        for s in range(NS):
            wvk = dma('gp', s_wb[0], wb[0][:, :], WkR[s],
                      waits=[(s_pe, wbuf_last_use[0])])
            wvv = dma('gp', s_wb[1], wb[1][:, :], WvR[s],
                      waits=[(s_pe, wbuf_last_use[1])])
            for hb in (1, 0):
                for wi, which in enumerate(("k", "v")):
                    bi = wi
                    wv = wvk if which == "k" else wvv
                    for t in range(NT):
                        gi = len(evac_counts)
                        pX = (pA, pB, pG, pH)[gi % 4]
                        waits = []
                        if t == 0:
                            waits.append((s_wb[wi], wv))
                        if t % 4 == 0:
                            waits.append(bnc_half[(hb, t // 4)])
                        if gi >= 4:
                            waits.append((s_ac, evac_counts[-4]))

                        def grp(e, t=t, bi=bi, pX=pX, srcT=srcT_of[hb]):
                            for c in range(NCH):
                                ins = e.matmul(
                                    pX[:, :],
                                    srcT[:, c * TL + t * P: c * TL + (t + 1) * P],
                                    wb[bi][:, c * 512:(c + 1) * 512],
                                    start=(c == 0), stop=(c == NCH - 1))
                            ins.then_inc(s_pe.h, 1)
                        step('pe', waits, grp)
                        gv = s_pe.bump()
                        wbuf_last_use[bi] = gv
                        dst = kvbuf[(which, hb)][:, t * 512:(t + 1) * 512]
                        func = AF.Relu if which == "k" else AF.Copy
                        evw = [(s_pe, gv)]
                        if kvks_last_mm[0] > 0:
                            evw.append((s_pe, kvks_last_mm[0]))
                        if which == "v" and hb == 0:
                            evw.extend((s_bw[(0, i)], 16) for i in range(NT))

                        def ev(e, dst=dst, func=func, pX=pX):
                            e.activation(dst, pX[:, :], func).then_inc(s_ac.h, 1)
                        step('ac', evw, ev)
                        evac_counts.append(s_ac.bump())
            slice_evacs = s_ac.n
            for half in range(2):
                pX = pC if half == 0 else pD
                waits = [(s_ac, slice_evacs)] if half == 0 else []
                if s == 0 and half == 0:
                    waits.append((s_gp, gp_const))
                if s > 0:
                    waits.append((s_ve, kvks_group_evac[(s - 1, half)]))

                def kvg(e, s=s, half=half, pX=pX):
                    first_mm = True
                    for ci in range(2):
                        off = ci * 258
                        cc = 2 * half + ci
                        for hb in (1, 0):
                            kH = kvbuf[("k", hb)]
                            vH = kvbuf[("v", hb)]
                            for t in range(NT):
                                e.matmul(pX[:, off:off + 128],
                                         kH[:, t * 512 + cc * 128: t * 512 + (cc + 1) * 128],
                                         vH[:, t * 512 + cc * 128: t * 512 + (cc + 1) * 128],
                                         start=first_mm, stop=False)
                                first_mm = False
                                ins = e.matmul(
                                    pX[:, off + 128:off + 129],
                                    kH[:, t * 512 + cc * 128: t * 512 + (cc + 1) * 128],
                                    ones_c[:, :],
                                    start=False,
                                    stop=(ci == 1 and hb == 0 and t == NT - 1))
                    ins.then_inc(s_pe.h, 1)
                step('pe', waits, kvg)
                gv = s_pe.bump()
                kvks_last_mm[0] = gv

                def kve(e, s=s, half=half, pX=pX):
                    for ci in range(2):
                        c = 4 * s + 2 * half + ci
                        ins = e.tensor_copy(kvks[:, c * 129:(c + 1) * 129],
                                            pX[:, ci * 258: ci * 258 + 129])
                    ins.then_inc(s_ve.h, 1)
                step('ve', [(s_pe, gv)], kve)
                kvks_group_evac[(s, half)] = s_ve.bump()

    phase_kv()
    kv1_ac_done = s_ac.n
    kvks_all_done = s_ve.n

    def kvq_build(e):
        for c in range(NCH):
            c0 = c * 130
            k0 = c * 129
            e.tensor_copy(kvq[0:64, c0:c0 + 64], kvks[0:64, k0:k0 + 64])
            e.tensor_copy(kvq[64:128, c0 + 64:c0 + 128], kvks[64:128, k0 + 64:k0 + 128])
            e.tensor_copy(kvq[0:64, c0 + 128:c0 + 129], kvks[0:64, k0 + 128:k0 + 129])
            ins = e.tensor_copy(kvq[64:128, c0 + 129:c0 + 130],
                                kvks[64:128, k0 + 128:k0 + 129])
        ins.then_inc(s_ve.h, 1)
    step('ve', [(s_ve, kvks_all_done), (s_gp, gp_const)], kvq_build)
    kvq_done = s_ve.bump()

    for t in range(NT):
        def rg(e, t=t):
            e.matmul(pG[:, t * 8:(t + 1) * 8], ones_r[:, :], rb_sb[:, :],
                     start=True, stop=False)
            for c in range(NCH):
                ins = e.matmul(pG[:, t * 8:(t + 1) * 8],
                               xnT[:, c * TL + t * P: c * TL + (t + 1) * P],
                               rW_sb[:, c * 8:(c + 1) * 8],
                               start=False, stop=(c == NCH - 1))
            ins.then_inc(s_pe.h, 1)
        step('pe', [(s_const, w_const), (s_ac, kv1_ac_done), (s_gp, gp_const)], rg)
        s_pe.bump()
    rt_pe_done = s_pe.n

    nwb_evac = 0
    for r in range(4):
        waits = [(s_row[1], nwv), (s_gp, gp_const), (s_ac, kv1_ac_done)]
        if r > 0:
            waits.append((s_ve, nwb_evac))

        def nb(e, r=r):
            e.matmul(pH[:, :], ones_r[:, :], row_b[:, r * 512:(r + 1) * 512],
                     start=True, stop=True).then_inc(s_pe.h, 1)
        step('pe', waits, nb)
        gv = s_pe.bump()

        def nbe(e, r=r):
            e.tensor_copy(nwb[:, r * 512:(r + 1) * 512], pH[:, :]).then_inc(s_ve.h, 1)
        step('ve', [(s_pe, gv)], nbe)
        nwb_evac = s_ve.bump()
    nwb_done = nwb_evac
    nwb_pe_last = s_pe.n

    for t in range(NT):
        gcol = gates[:, t * 8:(t + 1) * 8]
        l = pG[:, t * 8:(t + 1) * 8]

        def chain(emitfn, waits=()):
            step('ve', list(waits) + [(s_ve, s_ve.n)],
                 lambda e, f=emitfn: f(e).then_inc(s_ve.h, 1))
            return s_ve.bump()

        chain(lambda e, l=l: e.reduce_max(g_sc1[:, :], l, axis=AX, negate=True),
              waits=[(s_pe, rt_pe_done)])
        mskv = chain(lambda e, l=l: e.tensor_scalar(g_msk[:, :], l, g_sc1[:, :],
                                                    None, ALU.add))
        step('ac', [(s_ve, mskv)],
             lambda e: e.activation(g_exp[:, :], g_msk[:, :], AF.Exp).then_inc(s_ac.h, 1))
        expv = s_ac.bump()
        chain(lambda e: e.tensor_scalar(g_neg[:, :], g_msk[:, :], 0.0, None, ALU.is_ge))
        chain(lambda e: e.scalar_tensor_tensor(g_neg[:, :], g_neg[:, :], -BIG,
                                               g_msk[:, :], ALU.mult, ALU.add))
        chain(lambda e: e.reduce_max(g_sc2[:, :], g_neg[:, :], axis=AX))
        chain(lambda e: e.tensor_scalar(g_neg[:, :], g_msk[:, :], g_sc2[:, :],
                                        None, ALU.is_ge))
        chain(lambda e, gcol=gcol: e.tensor_tensor(gcol, g_neg[:, :], g_exp[:, :],
                                                   ALU.mult),
              waits=[(s_ac, expv)])
        chain(lambda e, gcol=gcol: e.reduce_sum(g_sc1[:, :], gcol, axis=AX))
        chain(lambda e: e.reciprocal(g_sc3[:, :], g_sc1[:, :]))
        chain(lambda e, gcol=gcol: e.tensor_scalar(gcol, gcol, g_sc3[:, :],
                                                   None, ALU.mult))
    gates_done = s_ve.n

    q_evacs = []
    for jj in range(4):
        bi = jj % 2
        for j4 in range(4):
            wv = dma('gp', s_wb[bi], wb[bi][:, j4 * D:(j4 + 1) * D], WqR[jj * 4 + j4],
                     waits=[(s_pe, wbuf_last_use[bi])] if j4 == 0 else ())
        for j4 in range(4):
            j = jj * 4 + j4
            for th in range(2):
                gi = len(q_evacs)
                pX = pA if gi % 2 == 0 else pB
                waits = [(s_wb[bi], wv)] if (j4 == 0 and th == 0) else []
                waits.append((s_ac, q_evacs[-2] if gi >= 2 else kv1_ac_done))

                def qg(e, j4=j4, th=th, bi=bi, pX=pX):
                    for c in range(NCH):
                        ins = e.matmul(
                            pX[:, :],
                            wb[bi][:, j4 * D + c * P: j4 * D + (c + 1) * P],
                            xnT[:, c * TL + th * 512: c * TL + (th + 1) * 512],
                            start=(c == 0), stop=(c == NCH - 1))
                    ins.then_inc(s_pe.h, 1)
                step('pe', waits, qg)
                gv = s_pe.bump()
                wbuf_last_use[bi] = gv

                def qe(e, j=j, th=th, pX=pX):
                    e.activation(qT[:, j * TL + th * 512: j * TL + (th + 1) * 512],
                                 pX[:, :], AF.Relu).then_inc(s_ac.h, 1)
                step('ac', [(s_pe, gv)], qe)
                q_evacs.append(s_ac.bump())
    q_done = s_ac.n

    div_done = []
    rec_done = {}
    norm_t_done = {}
    at_gi = 0
    for t in range(NT):
        pN = pC if t % 2 == 0 else pG
        nw_waits = ([(s_ve, kvq_done), (s_ac, q_done), (s_ve, gates_done)]
                    if t == 0 else [])
        if t >= 2:
            nw_waits.append((s_ve, rec_done[t - 2]))

        def ng(e, t=t, pN=pN):
            for c in range(NCH):
                ins = e.matmul(pN[:, c * 2: c * 2 + 2],
                               qT[:, c * TL + t * P: c * TL + (t + 1) * P],
                               kvq[:, c * 130 + 128: c * 130 + 130],
                               start=True, stop=True)
            ins.then_inc(s_pe.h, 1)
        step('pe', nw_waits, ng)
        ngv = s_pe.bump()
        step('ve', [(s_pe, ngv), (s_ve, s_ve.n)],
             lambda e, pN=pN: e.tensor_scalar(rec1[:, :], pN[:, 0:32], EPS_CLIP,
                                              None, ALU.max).then_inc(s_ve.h, 1))
        c0 = s_ve.bump()
        step('ve', [(s_ve, c0)],
             lambda e: e.reciprocal(rec2[:, :], rec1[:, :]).then_inc(s_ve.h, 1))
        rec_done[t] = s_ve.bump()
        for cg in range(4):
            pX = (pD, pH, pA, pB)[at_gi % 4]
            waits = [(s_ve, kvq_done)] if at_gi == 0 else []
            if at_gi >= 4:
                waits.append((s_ve, div_done[-4]))
            elif at_gi == 1:
                waits.append((s_ve, nwb_done))
            elif at_gi in (2, 3):
                waits.append((s_ac, q_done))

            def ag(e, t=t, cg=cg, pX=pX):
                for ci in range(4):
                    c = cg * 4 + ci
                    ins = e.matmul(pX[:, ci * P:(ci + 1) * P],
                                   qT[:, c * TL + t * P: c * TL + (t + 1) * P],
                                   kvq[:, c * 130: c * 130 + 128],
                                   start=True, stop=True)
                ins.then_inc(s_pe.h, 1)
            step('pe', waits, ag)
            gv = s_pe.bump()

            def dv(e, t=t, cg=cg, pX=pX):
                for ci in range(4):
                    c = cg * 4 + ci
                    for h in range(2):
                        ins = e.tensor_scalar(
                            qT[:, c * TL + t * P + h * 64: c * TL + t * P + (h + 1) * 64],
                            pX[:, ci * P + h * 64: ci * P + (h + 1) * 64],
                            rec2[:, c * 2 + h: c * 2 + h + 1], None, ALU.mult)
                ins.then_inc(s_ve.h, 1)
            step('ve', [(s_pe, gv), (s_ve, rec_done[t])], dv)
            div_done.append(s_ve.bump())
            at_gi += 1
        norm_t_done[t] = s_ve.n

    eb_last_use = [0, nwb_pe_last]
    comb_done = {}
    ex_silu = []
    ex_stt = []
    ex_gi = 0
    for e_i in range(NE):
        erow = row_a if e_i % 2 == 0 else row_b
        ebv = dma('gp', s_row[e_i % 2], erow[:, :], ebias[e_i:e_i + 1, :],
                  waits=[(s_pe, eb_last_use[e_i % 2])])
        for s in range(NS):
            bi = (e_i * NS + s) % 2
            wv = dma('gp', s_wb[bi], wb[bi][:, :], WeR[e_i, s],
                     waits=[(s_pe, wbuf_last_use[bi])])
            for t in range(NT):
                pX = pA if ex_gi % 2 == 0 else pB
                waits = ([(s_wb[bi], wv), (s_row[e_i % 2], ebv)]
                         if t == 0 else [])
                if ex_gi >= 2:
                    waits.append((s_ac, ex_silu[-2]))
                else:
                    waits.append((s_ac, q_done))
                    waits.append((s_ve, div_done[-1]))

                def eg(e, s=s, t=t, bi=bi, pX=pX, erow=erow):
                    e.matmul(pX[:, :], ones_r[:, :], erow[:, s * 512:(s + 1) * 512],
                             start=True, stop=False)
                    for c in range(NCH):
                        ins = e.matmul(
                            pX[:, :],
                            xnT[:, c * TL + t * P: c * TL + (t + 1) * P],
                            wb[bi][:, c * 512:(c + 1) * 512],
                            start=False, stop=(c == NCH - 1))
                    ins.then_inc(s_pe.h, 1)
                step('pe', waits, eg)
                gv = s_pe.bump()
                wbuf_last_use[bi] = gv
                eb_last_use[e_i % 2] = gv
                sw = [(s_pe, gv)]
                if ex_gi >= 2:
                    sw.append((s_ve, ex_stt[-2]))

                def sl(e, ex_gi=ex_gi, pX=pX):
                    e.activation(silu_t[ex_gi % 2][:, :], pX[:, :],
                                 AF.Silu).then_inc(s_ac.h, 1)
                step('ac', sw, sl)
                av = s_ac.bump()
                ex_silu.append(av)
                dst = comb[:, t * D + s * 512: t * D + (s + 1) * 512]
                gcolap = gates[:, t * 8 + e_i: t * 8 + e_i + 1]

                def cb(e, e_i=e_i, dst=dst, gcolap=gcolap, ex_gi=ex_gi):
                    srcb = silu_t[ex_gi % 2][:, :]
                    if e_i == 0:
                        ins = e.tensor_scalar(dst, srcb, gcolap, None, ALU.mult)
                    else:
                        ins = e.scalar_tensor_tensor(dst, srcb, gcolap, dst,
                                                     ALU.mult, ALU.add)
                    ins.then_inc(s_ve.h, 1)
                step('ve', [(s_ac, av), (s_ve, gates_done)], cb)
                ex_stt.append(s_ve.bump())
                if e_i == NE - 1 and s == NS - 1:
                    comb_done[t] = s_ve.n
                ex_gi += 1
    ex_ac_done = s_ac.n

    xnT_view = xnT[:, :].rearrange("p (c x) -> p c x", c=NCH)

    def emit_transpose_phase(src_done, slicer, prev_pe_pf=0):
        evacs = []
        done = {}
        ti = 0
        for t in range(NT):
            for ch in range(2):
                pX = pE if ti % 2 == 0 else pF
                waits = [(s_ve, src_done[t]), (s_x, x_ident)]
                waits.append((s_ve, evacs[-2] if ti >= 2 else prev_pe_pf))

                def tg(e, t=t, ch=ch, pX=pX):
                    for ci in range(8):
                        c = ch * 8 + ci
                        ins = e.transpose(pX[:, ci * P:(ci + 1) * P],
                                          slicer(t, c),
                                          id_sb[:, :])
                    ins.then_inc(s_pe.h, 1)
                step('pe', waits, tg)
                gv = s_pe.bump()

                def te(e, t=t, ch=ch, pX=pX):
                    dstv = xnT_view[:, ch * 8:(ch + 1) * 8, t * P:(t + 1) * P]
                    srcv = pX[:, :].rearrange("p (c x) -> p c x", c=8)
                    e.tensor_copy(dstv, srcv).then_inc(s_ve.h, 1)
                step('ve', [(s_pe, gv)], te)
                evacs.append(s_ve.bump())
                ti += 1
            done[t] = s_ve.n
        return done

    tc_done = emit_transpose_phase(
        comb_done, lambda t, c: comb[:, t * D + c * P: t * D + (c + 1) * P])

    pjbv = dma('gp', s_row[0], row_a[:, :], pjbias.ap(), waits=[(s_pe, eb_last_use[0])])
    pj_y = []
    restream = {}
    pj_gi = 0
    for s in range(NS):
        bi = s % 2
        wv = dma('gp', s_wb[bi], wb[bi][:, :], PjR[s], waits=[(s_pe, wbuf_last_use[bi])])
        for t in range(NT):
            pX = pA if pj_gi % 2 == 0 else pB
            waits = ([(s_wb[bi], wv), (s_row[0], pjbv)] if t == 0 else [])
            waits.append((s_ve, tc_done[t]))
            if pj_gi >= 2:
                waits.append((s_ve, pj_y[-2]))
            else:
                waits.append((s_ac, ex_ac_done))

            def pg(e, s=s, t=t, bi=bi, pX=pX):
                e.matmul(pX[:, :], ones_r[:, :], row_a[:, s * 512:(s + 1) * 512],
                         start=True, stop=False)
                for c in range(NCH):
                    ins = e.matmul(
                        pX[:, :],
                        xnT[:, c * TL + t * P: c * TL + (t + 1) * P],
                        wb[bi][:, c * 512:(c + 1) * 512],
                        start=False, stop=(c == NCH - 1))
                ins.then_inc(s_pe.h, 1)
            step('pe', waits, pg)
            gv = s_pe.bump()
            wbuf_last_use[bi] = gv
            xw = [(s_ve, pj_y[-2])] if len(pj_y) >= 2 else []
            xv = dma('sp', s_sl[pj_gi % 2], xn_sl[pj_gi % 2][:, :],
                     xn_dram[TL + t * P: TL + (t + 1) * P, s * 512:(s + 1) * 512],
                     waits=xw)
            dst = comb[:, t * D + s * 512: t * D + (s + 1) * 512]

            def ye(e, dst=dst, pX=pX, pj_gi=pj_gi):
                e.tensor_tensor(dst, pX[:, :], xn_sl[pj_gi % 2][:, :],
                                ALU.add).then_inc(s_ve.h, 1)
            step('ve', [(s_pe, gv), (s_sl[pj_gi % 2], xv)], ye)
            pj_y.append(s_ve.bump())
            pj_gi += 1
            if s == NS - 1:
                yv = s_ve.n
                xw = [(s_ve, restream[t - 2])] if t >= 2 else \
                     [(s_ve, ln_ts_done[(1, NT - 2 + t)])]
                xv2 = dma('sp', s_xa[t % 2], x_t[t % 2][:, :],
                          xa[t * P:(t + 1) * P, :], waits=xw)
                sqw = [(s_ve, yv)]
                if t >= 2:
                    sqw.append((s_ve, restream[t - 2]))

                def sqf(e, t=t):
                    e.activation(fo_t[t % 2][:, :], comb[:, t * D:(t + 1) * D],
                                 AF.Square, accum_out=sq[:, :]).then_inc(s_ac.h, 1)
                step('ac', sqw, sqf)
                v1 = s_ac.bump()
                step('ac', [(s_ac, v1)],
                     lambda e: e.activation(rmst[:, :], sq[:, :], AF.Sqrt,
                                            bias=c_epsrms[:, :],
                                            scale=c_invd[:, :]).then_inc(s_ac.h, 1))
                v2 = s_ac.bump()
                step('ve', [(s_ac, v2)],
                     lambda e: e.reciprocal(rrms[:, :], rmst[:, :]).then_inc(s_ve.h, 1))
                v3 = s_ve.bump()

                def fstt(e, t=t):
                    e.scalar_tensor_tensor(fo_t[t % 2][:, :], comb[:, t * D:(t + 1) * D],
                                           rrms[:, :], nwb[:, :], ALU.mult,
                                           ALU.mult).then_inc(s_ve.h, 1)
                step('ve', [(s_ve, v3), (s_ve, nwb_done)], fstt)
                v4 = s_ve.bump()

                def fadd(e, t=t):
                    e.tensor_tensor(comb[:, t * D:(t + 1) * D], fo_t[t % 2][:, :],
                                    x_t[t % 2][:, :], ALU.add).then_inc(s_ve.h, 1)
                step('ve', [(s_ve, v4), (s_xa[t % 2], xv2)], fadd)
                restream[t] = s_ve.bump()
    pj_pe_done = s_pe.n
    pj_act_done = s_ac.n
    pj_ve_done = s_ve.n

    bov = dma('gp', s_row[1], row_b[:, :], obias.ap(), waits=[(s_pe, eb_last_use[1])])

    am_done = emit_transpose_phase(
        norm_t_done, lambda t, c: qT[:, c * TL + t * P: c * TL + (t + 1) * P],
        prev_pe_pf=tc_done[NT - 1])

    wo_evacs = []
    wo_gi = 0
    for s in range(NS):
        bi = s % 2
        wv = dma('gp', s_wb[bi], wb[bi][:, :], WoR[s], waits=[(s_pe, wbuf_last_use[bi])])
        for t in range(NT):
            pX = pA if wo_gi % 2 == 0 else pB
            waits = ([(s_wb[bi], wv), (s_row[1], bov)] if t == 0 else [])
            waits.append((s_ve, am_done[t]))
            waits.append((s_ve, wo_evacs[-2] if wo_gi >= 2 else pj_ve_done))

            def wg(e, s=s, t=t, bi=bi, pX=pX):
                e.matmul(pX[:, :], ones_r[:, :], row_b[:, s * 512:(s + 1) * 512],
                         start=True, stop=False)
                for c in range(NCH):
                    ins = e.matmul(
                        pX[:, :],
                        xnT[:, c * TL + t * P: c * TL + (t + 1) * P],
                        wb[bi][:, c * 512:(c + 1) * 512],
                        start=False, stop=(c == NCH - 1))
                ins.then_inc(s_pe.h, 1)
            step('pe', waits, wg)
            gv = s_pe.bump()
            wbuf_last_use[bi] = gv
            sl_i = wo_gi % 2
            ew = [(s_pe, gv)]
            if wo_gi >= 2:
                ew.append((s_sta[sl_i], 16 * (wo_gi // 2)))

            def we(e, sl_i=sl_i, pX=pX, t=t, s=s):
                e.tensor_tensor(att_sl[sl_i][:, :], pX[:, :],
                                comb[:, t * D + s * 512: t * D + (s + 1) * 512],
                                ALU.add).then_inc(s_ve.h, 1)
            step('ve', ew, we)
            wo_evacs.append(s_ve.bump())
            dma('sp', s_sta[sl_i], out[t * P:(t + 1) * P, s * 512:(s + 1) * 512],
                att_sl[sl_i][:, :], waits=[(s_ve, s_ve.n)])
            wo_gi += 1

    step('sp', [(sm, sm.n) for sm in s_sta], lambda e: e.nop())

    with nc.Block() as block:
        @block.tensor
        def _(e):
            for k, fn in sched:
                if k == 'pe':
                    fn(e)

        @block.vector
        def _(e):
            for k, fn in sched:
                if k == 've':
                    fn(e)

        @block.scalar
        def _(e):
            for k, fn in sched:
                if k == 'ac':
                    fn(e)

        @block.gpsimd
        def _(e):
            for k, fn in sched:
                if k == 'gp':
                    fn(e)

        @block.sync
        def _(e):
            for k, fn in sched:
                if k == 'sp':
                    fn(e)

    es.close()
    return nc


_NC_CACHE = [None]


def kernel(L0_attn_hidden_states, L0_attn_Wq, L0_attn_Wk, L0_attn_Wv,
           L0_attn_Wo, L0_attn_bo, L0_ffn_router_w, L0_ffn_router_b,
           L0_ffn_expert_w, L0_ffn_expert_b, L0_ffn_proj_w, L0_ffn_proj_b,
           L0_ffn_norm_w):
    x = np.asarray(L0_attn_hidden_states, dtype=np.float32)
    B, S, _ = x.shape

    def mov(w):
        return np.ascontiguousarray(
            np.asarray(w, np.float32).reshape(NCH, P, NS, 512).transpose(2, 1, 0, 3)
        ).reshape(NS, P, NCH * 512)

    WqR = np.ascontiguousarray(
        np.asarray(L0_attn_Wq, np.float32).reshape(NCH, P, NCH, P).transpose(2, 1, 0, 3)
    ).reshape(NCH, P, D)
    shared = dict(
        WqR=WqR,
        WkR=mov(L0_attn_Wk),
        WvR=mov(L0_attn_Wv),
        WoR=mov(L0_attn_Wo),
        PjR=mov(L0_ffn_proj_w),
        WeR=np.stack([mov(np.asarray(L0_ffn_expert_w, np.float32)[e])
                      for e in range(NE)]),
        rWr=np.ascontiguousarray(
            np.asarray(L0_ffn_router_w, np.float32).reshape(NCH, P, 8).transpose(1, 0, 2)
        ).reshape(P, NCH * 8),
        rbias=np.asarray(L0_ffn_router_b, np.float32).reshape(1, 8),
        ebias=np.ascontiguousarray(np.asarray(L0_ffn_expert_b, np.float32)),
        pjbias=np.asarray(L0_ffn_proj_b, np.float32).reshape(1, D),
        obias=np.asarray(L0_attn_bo, np.float32).reshape(1, D),
        nwg=np.asarray(L0_ffn_norm_w, np.float32).reshape(1, D),
        ident=np.eye(P, dtype=ml_dtypes.bfloat16).view(np.uint16),
    )

    in_maps = []
    for core in range(8):
        b = core // 2
        half = core % 2
        in_maps.append(dict(
            xa=np.ascontiguousarray(x[b, half * TL:(half + 1) * TL, :]),
            xb=np.ascontiguousarray(x[b, (1 - half) * TL:(2 - half) * TL, :]),
            **shared))

    if _NC_CACHE[0] is None:
        _NC_CACHE[0] = build_nc()
    res = run_bass_kernel_spmd(_NC_CACHE[0], in_maps, core_ids=list(range(8)))

    outp = np.empty((B, S, D), dtype=np.float32)
    for core in range(8):
        b = core // 2
        half = core % 2
        outp[b, half * TL:(half + 1) * TL, :] = res.results[core]["out"]
    return outp


# version 27
# speedup vs baseline: 1.0122x; 1.0069x over previous
import numpy as np
import ml_dtypes
import concourse.bass as bass
import concourse.mybir as mybir
from concourse.bass_utils import run_bass_kernel_spmd
from contextlib import ExitStack

F32 = mybir.dt.float32
BF16 = mybir.dt.bfloat16
AF = mybir.ActivationFunctionType
ALU = mybir.AluOpType
AX = mybir.AxisListType.X

P = 128
D = 2048
NCH = 16
TL = 1024
NT = 8
NS = 4
NE = 8
EPS_LN = 1e-5
EPS_RMS = 1e-6
EPS_CLIP = 1e-6
BIG = 1e30


class Sem:
    def __init__(self, h):
        self.h = h
        self.n = 0

    def bump(self, k=1):
        self.n += k
        return self.n


def build_nc():
    nc = bass.Bass("TRN2", target_bir_lowering=False, debug=False)

    xa = nc.dram_tensor("xa", [TL, D], F32, kind="ExternalInput")
    xb = nc.dram_tensor("xb", [TL, D], F32, kind="ExternalInput")
    WqR = nc.dram_tensor("WqR", [NCH, P, D], F32, kind="ExternalInput")
    WkR = nc.dram_tensor("WkR", [NS, P, NCH * 512], F32, kind="ExternalInput")
    WvR = nc.dram_tensor("WvR", [NS, P, NCH * 512], F32, kind="ExternalInput")
    WoR = nc.dram_tensor("WoR", [NS, P, NCH * 512], F32, kind="ExternalInput")
    PjR = nc.dram_tensor("PjR", [NS, P, NCH * 512], F32, kind="ExternalInput")
    WeR = nc.dram_tensor("WeR", [NE, NS, P, NCH * 512], F32, kind="ExternalInput")
    rWr = nc.dram_tensor("rWr", [P, NCH * 8], F32, kind="ExternalInput")
    rbias = nc.dram_tensor("rbias", [1, 8], F32, kind="ExternalInput")
    ebias = nc.dram_tensor("ebias", [NE, D], F32, kind="ExternalInput")
    pjbias = nc.dram_tensor("pjbias", [1, D], F32, kind="ExternalInput")
    obias = nc.dram_tensor("obias", [1, D], F32, kind="ExternalInput")
    nwg = nc.dram_tensor("nwg", [1, D], F32, kind="ExternalInput")
    ident = nc.dram_tensor("ident", [P, P], BF16, kind="ExternalInput")
    out = nc.dram_tensor("out", [TL, D], F32, kind="ExternalOutput")

    xn_dram = nc.dram_tensor("xn_dram", [2 * TL, D], BF16)

    es = ExitStack()
    sbt = lambda name, shape, dt: es.enter_context(nc.sbuf_tensor(name, shape, dt))
    pst = lambda name, shape, dt: es.enter_context(nc.psum_tensor(name, shape, dt))

    xnT = sbt("xnT", [P, NCH * TL], BF16)
    qT = sbt("qT", [P, NCH * TL], BF16)
    comb = sbt("comb", [P, NT * D], BF16)
    wb = [sbt("wb%d" % i, [P, NCH * 512], BF16) for i in range(2)]
    x_t = [sbt("x_t%d" % i, [P, D], F32) for i in range(2)]
    fo_t = [sbt("fo_t%d" % i, [P, D], F32) for i in range(2)]
    xn_t = [sbt("xn_t%d" % i, [P, D], BF16) for i in range(2)]
    kvks = sbt("kvks", [P, NCH * 129], BF16)
    kvq = sbt("kvq", [P, NCH * 130], BF16)
    nwb = sbt("nwb", [P, D], BF16)
    rW_sb = sbt("rW_sb", [P, NCH * 8], BF16)
    row_a = sbt("row_a", [1, D], BF16)
    row_b = sbt("row_b", [1, D], BF16)
    rb_sb = sbt("rb_sb", [1, 8], BF16)
    silu_t = [sbt("silu%d" % i, [P, 512], BF16) for i in range(2)]
    xn_sl = [sbt("xnsl%d" % i, [P, 512], BF16) for i in range(2)]
    att_sl = [sbt("attsl%d" % i, [P, 512], F32) for i in range(2)]
    id_sb = sbt("id_sb", [P, P], BF16)
    ones_r = sbt("ones_r", [1, P], BF16)
    ones_c = sbt("ones_c", [P, 1], BF16)
    gates = sbt("gates", [P, NT * 8], F32)
    bst = [sbt("bst%d" % i, [P, 24], F32) for i in range(2)]
    mv = [sbt("mv%d" % i, [P, 2], F32) for i in range(2)]
    rstd1 = [sbt("rstd1_%d" % i, [P, 1], F32) for i in range(2)]
    rstd2 = [sbt("rstd2_%d" % i, [P, 1], F32) for i in range(2)]
    g_msk = sbt("g_msk", [P, 8], F32)
    g_neg = sbt("g_neg", [P, 8], F32)
    g_exp = sbt("g_exp", [P, 8], F32)
    g_sc1 = sbt("g_sc1", [P, 1], F32)
    g_sc2 = sbt("g_sc2", [P, 1], F32)
    g_sc3 = sbt("g_sc3", [P, 1], F32)
    sq = sbt("sq", [P, 1], F32)
    rmst = sbt("rmst", [P, 1], F32)
    rrms = sbt("rrms", [P, 1], F32)
    rec1 = sbt("rec1", [P, 32], F32)
    rec2 = sbt("rec2", [P, 32], F32)
    c_epsln = sbt("c_epsln", [P, 1], F32)
    c_epsrms = sbt("c_epsrms", [P, 1], F32)
    c_invd = sbt("c_invd", [P, 1], F32)

    kS = comb[:, 0: NT * 512]
    vS = comb[:, NT * 512: 2 * NT * 512]

    pA = pst("pA", [P, 512], F32)
    pB = pst("pB", [P, 512], F32)
    pC = pst("pC", [P, 512], F32)
    pD = pst("pD", [P, 512], F32)
    pE = pst("pE", [P, 1024], BF16)
    pF = pst("pF", [P, 1024], BF16)
    pG = pst("pG", [P, 512], F32)
    pH = pst("pH", [P, 512], F32)

    s_x = Sem(es.enter_context(nc.semaphore("s_x")))
    s_xa = [Sem(es.enter_context(nc.semaphore("s_xa%d" % i))) for i in range(2)]
    s_wb = [Sem(es.enter_context(nc.semaphore("s_wb%d" % i))) for i in range(2)]
    s_row = [Sem(es.enter_context(nc.semaphore("s_row%d" % i))) for i in range(2)]
    s_sl = [Sem(es.enter_context(nc.semaphore("s_sl%d" % i))) for i in range(2)]
    s_const = Sem(es.enter_context(nc.semaphore("s_const")))
    s_bnc = Sem(es.enter_context(nc.semaphore("s_bnc")))
    s_pe = Sem(es.enter_context(nc.semaphore("s_pe")))
    s_ac = Sem(es.enter_context(nc.semaphore("s_ac")))
    s_ve = Sem(es.enter_context(nc.semaphore("s_ve")))
    s_gp = Sem(es.enter_context(nc.semaphore("s_gp")))
    s_sta = [Sem(es.enter_context(nc.semaphore("s_sta%d" % i))) for i in range(2)]
    s_xp2 = [Sem(es.enter_context(nc.semaphore("s_xp%d" % i))) for i in range(2)]
    s_xpg = [Sem(es.enter_context(nc.semaphore("s_xpg%d" % i))) for i in range(2)]
    s_bw = {(hb, t): Sem(es.enter_context(nc.semaphore("s_bw%d%d" % (hb, t))))
            for hb in range(2) for t in range(NT)}
    s_tr = {(hb, th): Sem(es.enter_context(nc.semaphore("s_tr%d%d" % (hb, th))))
            for hb in range(2) for th in range(2)}

    sched = []

    def step(eng, waits, fn):
        waits = tuple((s.h, v) for (s, v) in waits if v > 0)

        def run(e, fn=fn, waits=waits):
            for (h, v) in waits:
                e.wait_ge(h, v)
            fn(e)
        sched.append((eng, run))

    def dma(eng, sem, out_ap, in_ap, waits=(), **kw):
        c = sem.bump(16)

        def fn(e, out_ap=out_ap, in_ap=in_ap, kw=kw, sem=sem):
            e.dma_start(out=out_ap, in_=in_ap, **kw).then_inc(sem.h, 16)
        step(eng, waits, fn)
        return c

    step('gp', (), lambda e: e.memset(ones_r[:, :], 1.0).then_inc(s_gp.h, 1))
    s_gp.bump()
    step('gp', (), lambda e: e.memset(ones_c[:, :], 1.0).then_inc(s_gp.h, 1))
    s_gp.bump()
    step('gp', (), lambda e: e.memset(kvq[:, :], 0.0).then_inc(s_gp.h, 1))
    s_gp.bump()
    step('gp', (), lambda e: e.memset(c_epsln[:, :], EPS_LN).then_inc(s_gp.h, 1))
    s_gp.bump()
    step('gp', (), lambda e: e.memset(c_epsrms[:, :], EPS_RMS).then_inc(s_gp.h, 1))
    s_gp.bump()
    step('gp', (), lambda e: e.memset(c_invd[:, :], 1.0 / D).then_inc(s_gp.h, 1))
    s_gp.bump()
    gp_const = s_gp.n
    x_ident = dma('sp', s_x, id_sb[:, :], ident.ap())
    dma('gp', s_const, rW_sb[:, :], rWr.ap())
    dma('gp', s_const, rb_sb[:, :], rbias.ap())
    nwv = dma('gp', s_row[1], row_b[:, :], nwg.ap())
    w_const = s_const.n

    ln_ts_done = {}
    bnc_half = {}
    xnT0_last_use = [0]

    xn_p = [comb[:, 12288 + i * D: 12288 + (i + 1) * D] for i in range(2)]

    ln_x_free = {}

    def ln_tile(hb, t, xbuf, xnbuf, xv, copy_extra=(), xsem=None):
        bsth, mvh, r1h, r2h = bst[hb], mv[hb], rstd1[hb], rstd2[hb]

        def bn(e, xbuf=xbuf, bsth=bsth):
            for i in range(4):
                ins = e.bn_stats(bsth[:, 6 * i:6 * i + 6],
                                 xbuf[:, 512 * i:512 * (i + 1)])
            ins.then_inc(s_ve.h, 1)
        step('ve', [(xsem, xv)], bn)
        bnv = s_ve.bump()
        step('ve', [(s_ve, bnv)],
             lambda e, mvh=mvh, bsth=bsth: e.bn_aggr(mvh[:, :], bsth[:, :]).then_inc(s_ve.h, 1))
        agv = s_ve.bump()
        step('ac', [(s_ve, agv), (s_gp, gp_const)],
             lambda e, mvh=mvh, r1h=r1h: e.activation(r1h[:, :], mvh[:, 1:2], AF.Sqrt,
                                                      bias=c_epsln[:, :]).then_inc(s_ac.h, 1))
        acv = s_ac.bump()
        step('ve', [(s_ac, acv)],
             lambda e, r1h=r1h, r2h=r2h: e.reciprocal(r2h[:, :], r1h[:, :]).then_inc(s_ve.h, 1))
        rv = s_ve.bump()

        def xnts(e, xbuf=xbuf, xnbuf=xnbuf, mvh=mvh, r2h=r2h):
            e.tensor_scalar(xnbuf, xbuf[:, :], mvh[:, 0:1],
                            r2h[:, :], ALU.subtract, ALU.mult).then_inc(s_ve.h, 1)
        step('ve', [(s_ve, rv)] + list(copy_extra), xnts)
        ln_ts_done[(hb, t)] = s_ve.bump()
        ln_x_free[(hb, t)] = ln_ts_done[(hb, t)]

    def emit_treads(hb, dstT, th, queue, c0, c1):
        ww = [(s_bw[(hb, 4 * th + i)], 16) for i in range(4)]
        for c in range(c0, c1):
            dma(queue, s_tr[(hb, th)],
                dstT[:, c * TL + th * 512: c * TL + (th + 1) * 512],
                xn_dram[hb * TL + th * 512: hb * TL + (th + 1) * 512,
                        c * P:(c + 1) * P],
                waits=ww, transpose=True)
        bnc_half[(hb, th)] = (s_tr[(hb, th)], 16 * NCH)

    for t in range(NT):
        w = [(s_ve, ln_x_free[(1, t - 2)])] if t >= 2 else []
        xv = dma('sp', s_xa[t % 2], x_t[t % 2][:, :], xa[t * P:(t + 1) * P, :],
                 waits=w)
        ce = [(s_bw[(1, t - 2)], 16)] if t >= 2 else []
        ln_tile(1, t, x_t[t % 2], xn_t[t % 2][:, :], xv, copy_extra=ce,
                xsem=s_xa[t % 2])
        dma('ac', s_bw[(1, t)],
            xn_dram[TL + t * P: TL + (t + 1) * P, :],
            xn_t[t % 2][:, :], waits=[(s_ve, s_ve.n)])
    xp_loads = {}
    for t in range(2):
        xp_loads[t] = dma('gp', s_xpg[t], fo_t[t][:, :], xb[t * P:(t + 1) * P, :])
    emit_treads(1, xnT, 0, 'sp', 0, 16)
    emit_treads(1, xnT, 1, 'sp', 0, 16)
    for t in range(NT):
        ce = [(s_bw[(0, t - 2)], 16)] if t >= 2 else []
        ln_tile(0, t, fo_t[t % 2], xn_p[t % 2], xp_loads[t], copy_extra=ce,
                xsem=(s_xpg[t] if t < 2 else s_xp2[t % 2]))
        if t + 2 < NT:
            xp_loads[t + 2] = dma('ac', s_xp2[t % 2], fo_t[t % 2][:, :],
                                  xb[(t + 2) * P:(t + 3) * P, :],
                                  waits=[(s_ve, ln_x_free[(0, t)])])
        dma('sp', s_bw[(0, t)],
            xn_dram[t * P:(t + 1) * P, :],
            xn_p[t % 2], waits=[(s_ve, s_ve.n)])
    emit_treads(0, qT, 0, 'sp', 0, 16)
    emit_treads(0, qT, 1, 'sp', 0, 16)

    wbuf_last_use = [0, 0]
    kvks_last_mm = [0]
    kvks_group_evac = {}
    kvbuf = {("k", 1): comb[:, 0:4096],
             ("v", 1): comb[:, 4096:8192],
             ("k", 0): comb[:, 8192:12288],
             ("v", 0): comb[:, 12288:16384]}
    srcT_of = {1: xnT, 0: qT}

    def phase_kv():
        evac_counts = []
        for s in range(NS):
            wvk = dma('gp', s_wb[0], wb[0][:, :], WkR[s],
                      waits=[(s_pe, wbuf_last_use[0])])
            wvv = dma('gp', s_wb[1], wb[1][:, :], WvR[s],
                      waits=[(s_pe, wbuf_last_use[1])])
            for hb in (1, 0):
                for wi, which in enumerate(("k", "v")):
                    bi = wi
                    wv = wvk if which == "k" else wvv
                    for t in range(NT):
                        gi = len(evac_counts)
                        pX = (pA, pB, pG, pH)[gi % 4]
                        waits = [(s_wb[wi], wv), bnc_half[(hb, t // 4)]]
                        if gi >= 4:
                            waits.append((s_ac, evac_counts[-4]))

                        def grp(e, t=t, bi=bi, pX=pX, srcT=srcT_of[hb]):
                            for c in range(NCH):
                                ins = e.matmul(
                                    pX[:, :],
                                    srcT[:, c * TL + t * P: c * TL + (t + 1) * P],
                                    wb[bi][:, c * 512:(c + 1) * 512],
                                    start=(c == 0), stop=(c == NCH - 1))
                            ins.then_inc(s_pe.h, 1)
                        step('pe', waits, grp)
                        gv = s_pe.bump()
                        wbuf_last_use[bi] = gv
                        dst = kvbuf[(which, hb)][:, t * 512:(t + 1) * 512]
                        func = AF.Relu if which == "k" else AF.Copy
                        evw = [(s_pe, gv)]
                        if kvks_last_mm[0] > 0:
                            evw.append((s_pe, kvks_last_mm[0]))
                        if which == "v" and hb == 0:
                            evw.extend((s_bw[(0, i)], 16) for i in range(NT))

                        def ev(e, dst=dst, func=func, pX=pX):
                            e.activation(dst, pX[:, :], func).then_inc(s_ac.h, 1)
                        step('ac', evw, ev)
                        evac_counts.append(s_ac.bump())
            slice_evacs = s_ac.n
            for half in range(2):
                pX = pC if half == 0 else pD
                waits = [(s_ac, slice_evacs), (s_gp, gp_const)]
                if s > 0:
                    waits.append((s_ve, kvks_group_evac[(s - 1, half)]))

                def kvg(e, s=s, half=half, pX=pX):
                    first_mm = True
                    for ci in range(2):
                        off = ci * 258
                        cc = 2 * half + ci
                        for hb in (1, 0):
                            kH = kvbuf[("k", hb)]
                            vH = kvbuf[("v", hb)]
                            for t in range(NT):
                                e.matmul(pX[:, off:off + 128],
                                         kH[:, t * 512 + cc * 128: t * 512 + (cc + 1) * 128],
                                         vH[:, t * 512 + cc * 128: t * 512 + (cc + 1) * 128],
                                         start=first_mm, stop=False)
                                first_mm = False
                                ins = e.matmul(
                                    pX[:, off + 128:off + 129],
                                    kH[:, t * 512 + cc * 128: t * 512 + (cc + 1) * 128],
                                    ones_c[:, :],
                                    start=False,
                                    stop=(ci == 1 and hb == 0 and t == NT - 1))
                    ins.then_inc(s_pe.h, 1)
                step('pe', waits, kvg)
                gv = s_pe.bump()
                kvks_last_mm[0] = gv

                def kve(e, s=s, half=half, pX=pX):
                    for ci in range(2):
                        c = 4 * s + 2 * half + ci
                        ins = e.tensor_copy(kvks[:, c * 129:(c + 1) * 129],
                                            pX[:, ci * 258: ci * 258 + 129])
                    ins.then_inc(s_ve.h, 1)
                step('ve', [(s_pe, gv)], kve)
                kvks_group_evac[(s, half)] = s_ve.bump()

    phase_kv()
    kv1_ac_done = s_ac.n
    kvks_all_done = s_ve.n

    def kvq_build(e):
        for c in range(NCH):
            c0 = c * 130
            k0 = c * 129
            e.tensor_copy(kvq[0:64, c0:c0 + 64], kvks[0:64, k0:k0 + 64])
            e.tensor_copy(kvq[64:128, c0 + 64:c0 + 128], kvks[64:128, k0 + 64:k0 + 128])
            e.tensor_copy(kvq[0:64, c0 + 128:c0 + 129], kvks[0:64, k0 + 128:k0 + 129])
            ins = e.tensor_copy(kvq[64:128, c0 + 129:c0 + 130],
                                kvks[64:128, k0 + 128:k0 + 129])
        ins.then_inc(s_ve.h, 1)
    step('ve', [(s_ve, kvks_all_done), (s_gp, gp_const)], kvq_build)
    kvq_done = s_ve.bump()

    q_evacs = []
    for jj in range(4):
        bi = jj % 2
        for j4 in range(4):
            wv = dma('gp', s_wb[bi], wb[bi][:, j4 * D:(j4 + 1) * D], WqR[jj * 4 + j4],
                     waits=[(s_pe, wbuf_last_use[bi])] if j4 == 0 else ())
        for j4 in range(4):
            j = jj * 4 + j4
            for th in range(2):
                gi = len(q_evacs)
                pX = pA if gi % 2 == 0 else pB
                waits = [(s_wb[bi], wv)]
                waits.append((s_ac, q_evacs[-2] if gi >= 2 else kv1_ac_done))

                def qg(e, j4=j4, th=th, bi=bi, pX=pX):
                    for c in range(NCH):
                        ins = e.matmul(
                            pX[:, :],
                            wb[bi][:, j4 * D + c * P: j4 * D + (c + 1) * P],
                            xnT[:, c * TL + th * 512: c * TL + (th + 1) * 512],
                            start=(c == 0), stop=(c == NCH - 1))
                    ins.then_inc(s_pe.h, 1)
                step('pe', waits, qg)
                gv = s_pe.bump()
                wbuf_last_use[bi] = gv

                def qe(e, j=j, th=th, pX=pX):
                    e.activation(qT[:, j * TL + th * 512: j * TL + (th + 1) * 512],
                                 pX[:, :], AF.Relu).then_inc(s_ac.h, 1)
                step('ac', [(s_pe, gv)], qe)
                q_evacs.append(s_ac.bump())
    q_done = s_ac.n

    for t in range(NT):
        def rg(e, t=t):
            e.matmul(pG[:, t * 8:(t + 1) * 8], ones_r[:, :], rb_sb[:, :],
                     start=True, stop=False)
            for c in range(NCH):
                ins = e.matmul(pG[:, t * 8:(t + 1) * 8],
                               xnT[:, c * TL + t * P: c * TL + (t + 1) * P],
                               rW_sb[:, c * 8:(c + 1) * 8],
                               start=False, stop=(c == NCH - 1))
            ins.then_inc(s_pe.h, 1)
        step('pe', [(s_const, w_const), (s_ac, kv1_ac_done), (s_gp, gp_const)], rg)
        s_pe.bump()
    rt_pe_done = s_pe.n

    nwb_evac = 0
    for r in range(4):
        waits = [(s_row[1], nwv), (s_gp, gp_const), (s_ac, kv1_ac_done)]
        if r > 0:
            waits.append((s_ve, nwb_evac))

        def nb(e, r=r):
            e.matmul(pH[:, :], ones_r[:, :], row_b[:, r * 512:(r + 1) * 512],
                     start=True, stop=True).then_inc(s_pe.h, 1)
        step('pe', waits, nb)
        gv = s_pe.bump()

        def nbe(e, r=r):
            e.tensor_copy(nwb[:, r * 512:(r + 1) * 512], pH[:, :]).then_inc(s_ve.h, 1)
        step('ve', [(s_pe, gv)], nbe)
        nwb_evac = s_ve.bump()
    nwb_done = nwb_evac
    nwb_pe_last = s_pe.n

    for t in range(NT):
        gcol = gates[:, t * 8:(t + 1) * 8]
        l = pG[:, t * 8:(t + 1) * 8]

        def chain(emitfn, waits=()):
            step('ve', list(waits) + [(s_ve, s_ve.n)],
                 lambda e, f=emitfn: f(e).then_inc(s_ve.h, 1))
            return s_ve.bump()

        chain(lambda e, l=l: e.reduce_max(g_sc1[:, :], l, axis=AX, negate=True),
              waits=[(s_pe, rt_pe_done)])
        mskv = chain(lambda e, l=l: e.tensor_scalar(g_msk[:, :], l, g_sc1[:, :],
                                                    None, ALU.add))
        step('ac', [(s_ve, mskv)],
             lambda e: e.activation(g_exp[:, :], g_msk[:, :], AF.Exp).then_inc(s_ac.h, 1))
        expv = s_ac.bump()
        chain(lambda e: e.tensor_scalar(g_neg[:, :], g_msk[:, :], 0.0, None, ALU.is_ge))
        chain(lambda e: e.scalar_tensor_tensor(g_neg[:, :], g_neg[:, :], -BIG,
                                               g_msk[:, :], ALU.mult, ALU.add))
        chain(lambda e: e.reduce_max(g_sc2[:, :], g_neg[:, :], axis=AX))
        chain(lambda e: e.tensor_scalar(g_neg[:, :], g_msk[:, :], g_sc2[:, :],
                                        None, ALU.is_ge))
        chain(lambda e, gcol=gcol: e.tensor_tensor(gcol, g_neg[:, :], g_exp[:, :],
                                                   ALU.mult),
              waits=[(s_ac, expv)])
        chain(lambda e, gcol=gcol: e.reduce_sum(g_sc1[:, :], gcol, axis=AX))
        chain(lambda e: e.reciprocal(g_sc3[:, :], g_sc1[:, :]))
        chain(lambda e, gcol=gcol: e.tensor_scalar(gcol, gcol, g_sc3[:, :],
                                                   None, ALU.mult))
    gates_done = s_ve.n

    div_done = []
    rec_done = {}
    norm_t_done = {}
    at_gi = 0
    for t in range(NT):
        pN = pC if t % 2 == 0 else pG
        nw_waits = [(s_ve, kvq_done), (s_ac, q_done), (s_ve, gates_done)]
        if t >= 2:
            nw_waits.append((s_ve, rec_done[t - 2]))

        def ng(e, t=t, pN=pN):
            for c in range(NCH):
                ins = e.matmul(pN[:, c * 2: c * 2 + 2],
                               qT[:, c * TL + t * P: c * TL + (t + 1) * P],
                               kvq[:, c * 130 + 128: c * 130 + 130],
                               start=True, stop=True)
            ins.then_inc(s_pe.h, 1)
        step('pe', nw_waits, ng)
        ngv = s_pe.bump()
        step('ve', [(s_pe, ngv), (s_ve, s_ve.n)],
             lambda e, pN=pN: e.tensor_scalar(rec1[:, :], pN[:, 0:32], EPS_CLIP,
                                              None, ALU.max).then_inc(s_ve.h, 1))
        c0 = s_ve.bump()
        step('ve', [(s_ve, c0)],
             lambda e: e.reciprocal(rec2[:, :], rec1[:, :]).then_inc(s_ve.h, 1))
        rec_done[t] = s_ve.bump()
        for cg in range(4):
            pX = pD if at_gi % 2 == 0 else pH
            waits = [(s_ve, kvq_done)]
            if at_gi >= 2:
                waits.append((s_ve, div_done[-2]))
            elif at_gi == 1:
                waits.append((s_ve, nwb_done))

            def ag(e, t=t, cg=cg, pX=pX):
                for ci in range(4):
                    c = cg * 4 + ci
                    ins = e.matmul(pX[:, ci * P:(ci + 1) * P],
                                   qT[:, c * TL + t * P: c * TL + (t + 1) * P],
                                   kvq[:, c * 130: c * 130 + 128],
                                   start=True, stop=True)
                ins.then_inc(s_pe.h, 1)
            step('pe', waits, ag)
            gv = s_pe.bump()

            def dv(e, t=t, cg=cg, pX=pX):
                for ci in range(4):
                    c = cg * 4 + ci
                    for h in range(2):
                        ins = e.tensor_scalar(
                            qT[:, c * TL + t * P + h * 64: c * TL + t * P + (h + 1) * 64],
                            pX[:, ci * P + h * 64: ci * P + (h + 1) * 64],
                            rec2[:, c * 2 + h: c * 2 + h + 1], None, ALU.mult)
                ins.then_inc(s_ve.h, 1)
            step('ve', [(s_pe, gv), (s_ve, rec_done[t])], dv)
            div_done.append(s_ve.bump())
            at_gi += 1
        norm_t_done[t] = s_ve.n

    eb_last_use = [0, nwb_pe_last]
    comb_done = {}
    ex_silu = []
    ex_stt = []
    ex_gi = 0
    for e_i in range(NE):
        erow = row_a if e_i % 2 == 0 else row_b
        ebv = dma('gp', s_row[e_i % 2], erow[:, :], ebias[e_i:e_i + 1, :],
                  waits=[(s_pe, eb_last_use[e_i % 2])])
        for s in range(NS):
            bi = (e_i * NS + s) % 2
            wv = dma('gp', s_wb[bi], wb[bi][:, :], WeR[e_i, s],
                     waits=[(s_pe, wbuf_last_use[bi])])
            for t in range(NT):
                pX = pA if ex_gi % 2 == 0 else pB
                waits = [(s_wb[bi], wv), (s_row[e_i % 2], ebv)]
                waits.append((s_ac, ex_silu[-2] if ex_gi >= 2 else q_done))

                def eg(e, s=s, t=t, bi=bi, pX=pX, erow=erow):
                    e.matmul(pX[:, :], ones_r[:, :], erow[:, s * 512:(s + 1) * 512],
                             start=True, stop=False)
                    for c in range(NCH):
                        ins = e.matmul(
                            pX[:, :],
                            xnT[:, c * TL + t * P: c * TL + (t + 1) * P],
                            wb[bi][:, c * 512:(c + 1) * 512],
                            start=False, stop=(c == NCH - 1))
                    ins.then_inc(s_pe.h, 1)
                step('pe', waits, eg)
                gv = s_pe.bump()
                wbuf_last_use[bi] = gv
                eb_last_use[e_i % 2] = gv
                sw = [(s_pe, gv)]
                if ex_gi >= 2:
                    sw.append((s_ve, ex_stt[-2]))

                def sl(e, ex_gi=ex_gi, pX=pX):
                    e.activation(silu_t[ex_gi % 2][:, :], pX[:, :],
                                 AF.Silu).then_inc(s_ac.h, 1)
                step('ac', sw, sl)
                av = s_ac.bump()
                ex_silu.append(av)
                dst = comb[:, t * D + s * 512: t * D + (s + 1) * 512]
                gcolap = gates[:, t * 8 + e_i: t * 8 + e_i + 1]

                def cb(e, e_i=e_i, dst=dst, gcolap=gcolap, ex_gi=ex_gi):
                    srcb = silu_t[ex_gi % 2][:, :]
                    if e_i == 0:
                        ins = e.tensor_scalar(dst, srcb, gcolap, None, ALU.mult)
                    else:
                        ins = e.scalar_tensor_tensor(dst, srcb, gcolap, dst,
                                                     ALU.mult, ALU.add)
                    ins.then_inc(s_ve.h, 1)
                step('ve', [(s_ac, av), (s_ve, gates_done)], cb)
                ex_stt.append(s_ve.bump())
                if e_i == NE - 1 and s == NS - 1:
                    comb_done[t] = s_ve.n
                ex_gi += 1
    ex_ac_done = s_ac.n

    xnT_view = xnT[:, :].rearrange("p (c x) -> p c x", c=NCH)

    def emit_transpose_phase(src_done, slicer, prev_pe_pf=0):
        evacs = []
        done = {}
        ti = 0
        for t in range(NT):
            for ch in range(2):
                pX = pE if ti % 2 == 0 else pF
                waits = [(s_ve, src_done[t]), (s_x, x_ident)]
                waits.append((s_ve, evacs[-2] if ti >= 2 else prev_pe_pf))

                def tg(e, t=t, ch=ch, pX=pX):
                    for ci in range(8):
                        c = ch * 8 + ci
                        ins = e.transpose(pX[:, ci * P:(ci + 1) * P],
                                          slicer(t, c),
                                          id_sb[:, :])
                    ins.then_inc(s_pe.h, 1)
                step('pe', waits, tg)
                gv = s_pe.bump()

                def te(e, t=t, ch=ch, pX=pX):
                    dstv = xnT_view[:, ch * 8:(ch + 1) * 8, t * P:(t + 1) * P]
                    srcv = pX[:, :].rearrange("p (c x) -> p c x", c=8)
                    e.tensor_copy(dstv, srcv).then_inc(s_ve.h, 1)
                step('ve', [(s_pe, gv)], te)
                evacs.append(s_ve.bump())
                ti += 1
            done[t] = s_ve.n
        return done

    tc_done = emit_transpose_phase(
        comb_done, lambda t, c: comb[:, t * D + c * P: t * D + (c + 1) * P])

    pjbv = dma('gp', s_row[0], row_a[:, :], pjbias.ap(), waits=[(s_pe, eb_last_use[0])])
    pj_y = []
    restream = {}
    pj_gi = 0
    for s in range(NS):
        bi = s % 2
        wv = dma('gp', s_wb[bi], wb[bi][:, :], PjR[s], waits=[(s_pe, wbuf_last_use[bi])])
        for t in range(NT):
            pX = pA if pj_gi % 2 == 0 else pB
            waits = [(s_wb[bi], wv), (s_row[0], pjbv), (s_ve, tc_done[t])]
            if pj_gi >= 2:
                waits.append((s_ve, pj_y[-2]))
            else:
                waits.append((s_ac, ex_ac_done))

            def pg(e, s=s, t=t, bi=bi, pX=pX):
                e.matmul(pX[:, :], ones_r[:, :], row_a[:, s * 512:(s + 1) * 512],
                         start=True, stop=False)
                for c in range(NCH):
                    ins = e.matmul(
                        pX[:, :],
                        xnT[:, c * TL + t * P: c * TL + (t + 1) * P],
                        wb[bi][:, c * 512:(c + 1) * 512],
                        start=False, stop=(c == NCH - 1))
                ins.then_inc(s_pe.h, 1)
            step('pe', waits, pg)
            gv = s_pe.bump()
            wbuf_last_use[bi] = gv
            xw = [(s_ve, pj_y[-2])] if len(pj_y) >= 2 else []
            xv = dma('sp', s_sl[pj_gi % 2], xn_sl[pj_gi % 2][:, :],
                     xn_dram[TL + t * P: TL + (t + 1) * P, s * 512:(s + 1) * 512],
                     waits=xw)
            dst = comb[:, t * D + s * 512: t * D + (s + 1) * 512]

            def ye(e, dst=dst, pX=pX, pj_gi=pj_gi):
                e.tensor_tensor(dst, pX[:, :], xn_sl[pj_gi % 2][:, :],
                                ALU.add).then_inc(s_ve.h, 1)
            step('ve', [(s_pe, gv), (s_sl[pj_gi % 2], xv)], ye)
            pj_y.append(s_ve.bump())
            pj_gi += 1
            if s == NS - 1:
                yv = s_ve.n
                xw = [(s_ve, restream[t - 2])] if t >= 2 else \
                     [(s_ve, ln_ts_done[(1, NT - 2 + t)])]
                xv2 = dma('sp', s_xa[t % 2], x_t[t % 2][:, :],
                          xa[t * P:(t + 1) * P, :], waits=xw)
                sqw = [(s_ve, yv)]
                if t >= 2:
                    sqw.append((s_ve, restream[t - 2]))

                def sqf(e, t=t):
                    e.activation(fo_t[t % 2][:, :], comb[:, t * D:(t + 1) * D],
                                 AF.Square, accum_out=sq[:, :]).then_inc(s_ac.h, 1)
                step('ac', sqw, sqf)
                v1 = s_ac.bump()
                step('ac', [(s_ac, v1)],
                     lambda e: e.activation(rmst[:, :], sq[:, :], AF.Sqrt,
                                            bias=c_epsrms[:, :],
                                            scale=c_invd[:, :]).then_inc(s_ac.h, 1))
                v2 = s_ac.bump()
                step('ve', [(s_ac, v2)],
                     lambda e: e.reciprocal(rrms[:, :], rmst[:, :]).then_inc(s_ve.h, 1))
                v3 = s_ve.bump()

                def fstt(e, t=t):
                    e.scalar_tensor_tensor(fo_t[t % 2][:, :], comb[:, t * D:(t + 1) * D],
                                           rrms[:, :], nwb[:, :], ALU.mult,
                                           ALU.mult).then_inc(s_ve.h, 1)
                step('ve', [(s_ve, v3), (s_ve, nwb_done)], fstt)
                v4 = s_ve.bump()

                def fadd(e, t=t):
                    e.tensor_tensor(comb[:, t * D:(t + 1) * D], fo_t[t % 2][:, :],
                                    x_t[t % 2][:, :], ALU.add).then_inc(s_ve.h, 1)
                step('ve', [(s_ve, v4), (s_xa[t % 2], xv2)], fadd)
                restream[t] = s_ve.bump()
    pj_pe_done = s_pe.n
    pj_act_done = s_ac.n
    pj_ve_done = s_ve.n

    bov = dma('gp', s_row[1], row_b[:, :], obias.ap(), waits=[(s_pe, eb_last_use[1])])

    am_done = emit_transpose_phase(
        norm_t_done, lambda t, c: qT[:, c * TL + t * P: c * TL + (t + 1) * P],
        prev_pe_pf=tc_done[NT - 1])

    wo_evacs = []
    wo_gi = 0
    for s in range(NS):
        bi = s % 2
        wv = dma('gp', s_wb[bi], wb[bi][:, :], WoR[s], waits=[(s_pe, wbuf_last_use[bi])])
        for t in range(NT):
            pX = pA if wo_gi % 2 == 0 else pB
            waits = [(s_wb[bi], wv), (s_row[1], bov), (s_ve, am_done[t])]
            waits.append((s_ve, wo_evacs[-2] if wo_gi >= 2 else pj_ve_done))

            def wg(e, s=s, t=t, bi=bi, pX=pX):
                e.matmul(pX[:, :], ones_r[:, :], row_b[:, s * 512:(s + 1) * 512],
                         start=True, stop=False)
                for c in range(NCH):
                    ins = e.matmul(
                        pX[:, :],
                        xnT[:, c * TL + t * P: c * TL + (t + 1) * P],
                        wb[bi][:, c * 512:(c + 1) * 512],
                        start=False, stop=(c == NCH - 1))
                ins.then_inc(s_pe.h, 1)
            step('pe', waits, wg)
            gv = s_pe.bump()
            wbuf_last_use[bi] = gv
            sl_i = wo_gi % 2
            ew = [(s_pe, gv)]
            if wo_gi >= 2:
                ew.append((s_sta[sl_i], 16 * (wo_gi // 2)))

            def we(e, sl_i=sl_i, pX=pX, t=t, s=s):
                e.tensor_tensor(att_sl[sl_i][:, :], pX[:, :],
                                comb[:, t * D + s * 512: t * D + (s + 1) * 512],
                                ALU.add).then_inc(s_ve.h, 1)
            step('ve', ew, we)
            wo_evacs.append(s_ve.bump())
            dma('sp' if sl_i == 0 else 'ac', s_sta[sl_i],
                out[t * P:(t + 1) * P, s * 512:(s + 1) * 512],
                att_sl[sl_i][:, :], waits=[(s_ve, s_ve.n)])
            wo_gi += 1

    step('sp', [(sm, sm.n) for sm in s_sta], lambda e: e.nop())

    with nc.Block() as block:
        @block.tensor
        def _(e):
            for k, fn in sched:
                if k == 'pe':
                    fn(e)

        @block.vector
        def _(e):
            for k, fn in sched:
                if k == 've':
                    fn(e)

        @block.scalar
        def _(e):
            for k, fn in sched:
                if k == 'ac':
                    fn(e)

        @block.gpsimd
        def _(e):
            for k, fn in sched:
                if k == 'gp':
                    fn(e)

        @block.sync
        def _(e):
            for k, fn in sched:
                if k == 'sp':
                    fn(e)

    es.close()
    return nc


_NC_CACHE = [None]


def kernel(L0_attn_hidden_states, L0_attn_Wq, L0_attn_Wk, L0_attn_Wv,
           L0_attn_Wo, L0_attn_bo, L0_ffn_router_w, L0_ffn_router_b,
           L0_ffn_expert_w, L0_ffn_expert_b, L0_ffn_proj_w, L0_ffn_proj_b,
           L0_ffn_norm_w):
    x = np.asarray(L0_attn_hidden_states, dtype=np.float32)
    B, S, _ = x.shape

    def mov(w):
        return np.ascontiguousarray(
            np.asarray(w, np.float32).reshape(NCH, P, NS, 512).transpose(2, 1, 0, 3)
        ).reshape(NS, P, NCH * 512)

    WqR = np.ascontiguousarray(
        np.asarray(L0_attn_Wq, np.float32).reshape(NCH, P, NCH, P).transpose(2, 1, 0, 3)
    ).reshape(NCH, P, D)
    shared = dict(
        WqR=WqR,
        WkR=mov(L0_attn_Wk),
        WvR=mov(L0_attn_Wv),
        WoR=mov(L0_attn_Wo),
        PjR=mov(L0_ffn_proj_w),
        WeR=np.stack([mov(np.asarray(L0_ffn_expert_w, np.float32)[e])
                      for e in range(NE)]),
        rWr=np.ascontiguousarray(
            np.asarray(L0_ffn_router_w, np.float32).reshape(NCH, P, 8).transpose(1, 0, 2)
        ).reshape(P, NCH * 8),
        rbias=np.asarray(L0_ffn_router_b, np.float32).reshape(1, 8),
        ebias=np.ascontiguousarray(np.asarray(L0_ffn_expert_b, np.float32)),
        pjbias=np.asarray(L0_ffn_proj_b, np.float32).reshape(1, D),
        obias=np.asarray(L0_attn_bo, np.float32).reshape(1, D),
        nwg=np.asarray(L0_ffn_norm_w, np.float32).reshape(1, D),
        ident=np.eye(P, dtype=ml_dtypes.bfloat16).view(np.uint16),
    )

    in_maps = []
    for core in range(8):
        b = core // 2
        half = core % 2
        in_maps.append(dict(
            xa=np.ascontiguousarray(x[b, half * TL:(half + 1) * TL, :]),
            xb=np.ascontiguousarray(x[b, (1 - half) * TL:(2 - half) * TL, :]),
            **shared))

    if _NC_CACHE[0] is None:
        _NC_CACHE[0] = build_nc()
    res = run_bass_kernel_spmd(_NC_CACHE[0], in_maps, core_ids=list(range(8)))

    outp = np.empty((B, S, D), dtype=np.float32)
    for core in range(8):
        b = core // 2
        half = core % 2
        outp[b, half * TL:(half + 1) * TL, :] = res.results[core]["out"]
    return outp
